# version 07
# baseline (speedup 1.0000x reference)
import contextlib

import numpy as np
import ml_dtypes

import concourse.bass as bass
import concourse.mybir as mybir
from concourse.bass_utils import run_bass_kernel_spmd

F32 = mybir.dt.float32
BF16 = mybir.dt.bfloat16
AF = mybir.ActivationFunctionType
ALU = mybir.AluOpType
BF = ml_dtypes.bfloat16

D = 2048
S = 8192
NCORES = 8
TOWN = 1024
HFF = 8192
NB = 8
EPS = 1e-5
EXP_SHIFT = -8.0
FWHT_SCALE = float(1.0 / np.sqrt(np.float64(HFF)))

DEBUG = []
TRUNC = None


def own_tiles(c):
    return [8 * b + (c if b % 2 == 0 else 7 - c) for b in range(NB)]


ENGINES = ("sync", "gpsimd", "tensor", "scalar", "vector")


class Sem:
    __slots__ = ("h", "val")

    def __init__(self, handle):
        self.h = handle
        self.val = 0

    def bump(self, n):
        self.val += n
        return self.val


class Op:
    __slots__ = ("fn", "waits", "incs", "seq")

    def __init__(self, fn, waits, incs, seq):
        self.fn = fn
        self.waits = waits
        self.incs = incs
        self.seq = seq


class Sched:
    def __init__(self):
        self.ops = {e: [] for e in ENGINES}
        self.guards = {}
        self.seq = 0
        self.limit = None

    def add(self, eng, fn, waits=(), incs=()):
        seen = {}
        for s, v in waits:
            if v <= 0:
                continue
            if s.h.num in seen and seen[s.h.num][1] >= v:
                continue
            seen[s.h.num] = (s, v)
        self.ops[eng].append(Op(fn, list(seen.values()), list(incs), self.seq))
        self.seq += 1
        for s, n in incs:
            s.bump(n)

    def guard(self, key):
        g = self.guards.get(key)
        return [g] if g else []

    def set_guard(self, key, sem):
        self.guards[key] = (sem, sem.val)

    def trunc_totals(self):
        totals = {}
        for ops in self.ops.values():
            for op in ops:
                if op.seq < self.limit:
                    for s, n in op.incs:
                        totals[s] = totals.get(s, 0) + n
        return totals

    def emit(self, eng_name, eng):
        for op in self.ops[eng_name]:
            if self.limit is not None and op.seq >= self.limit:
                continue
            for s, v in op.waits:
                eng.wait_ge(s.h, v)
            if op.fn is None:
                continue
            ins = op.fn(eng)
            for s, n in op.incs:
                ins = ins.then_inc(s.h, n)
        if self.limit is not None:
            for s, total in self.trunc_totals().items():
                eng.wait_ge(s.h, total)


def hadamard128():
    h = np.ones((1, 1), dtype=np.float32)
    while h.shape[0] < 128:
        h = np.block([[h, h], [h, -h]])
    return h.astype(BF)


def build_program(debug=()):
    nc = bass.Bass("TRN2", target_bir_lowering=False)

    x_full = nc.dram_tensor("x_full", [S, D], F32, kind="ExternalInput").ap()
    x_own = nc.dram_tensor("x_own", [TOWN, D], F32, kind="ExternalInput").ap()
    wq_d = nc.dram_tensor("Wq", [D, D], F32, kind="ExternalInput").ap()
    wk_d = nc.dram_tensor("Wk", [D, 512], F32, kind="ExternalInput").ap()
    wv_d = nc.dram_tensor("Wv", [D, 512], F32, kind="ExternalInput").ap()
    wo_d = nc.dram_tensor("Wo", [D, D], F32, kind="ExternalInput").ap()
    w1_d = nc.dram_tensor("W1", [HFF, D], F32, kind="ExternalInput").ap()
    w2_d = nc.dram_tensor("W2", [D, HFF], F32, kind="ExternalInput").ap()
    w3_d = nc.dram_tensor("W3", [HFF, D], F32, kind="ExternalInput").ap()
    masks_d = nc.dram_tensor("masks", [2 * NB * 2 * 128, 128], BF16, kind="ExternalInput").ap()
    y_own = nc.dram_tensor("y_own", [TOWN, D], F32, kind="ExternalOutput").ap()

    w1_bf = nc.dram_tensor("w1_bf", [HFF, D], BF16, kind="Internal").ap()
    w3_bf = nc.dram_tensor("w3_bf", [HFF, D], BF16, kind="Internal").ap()
    w2_bf = nc.dram_tensor("w2_bf", [D, HFF], BF16, kind="Internal").ap()
    w2T_bf = nc.dram_tensor("w2T_bf", [HFF, D], BF16, kind="Internal").ap()
    facc_dram = nc.dram_tensor("facc_dram", [TOWN, D], F32, kind="Internal").ap()
    kTd = nc.dram_tensor("kTd", [NB * 128, 4 * 1024], BF16, kind="Internal").ap()
    vhd = nc.dram_tensor("vhd", [NB * 128, 8 * 520], BF16, kind="Internal").ap()
    raw_dram = nc.dram_tensor("raw_dram", [TOWN, D], F32, kind="Internal").ap()
    x2_dram = nc.dram_tensor("x2_dram", [TOWN, D], F32, kind="Internal").ap()
    xn2t_dram = nc.dram_tensor("xn2t_dram", [D, TOWN], BF16, kind="Internal").ap()

    dbg = {}

    def dbg_out(name, shape, dtype=F32):
        dbg[name] = nc.dram_tensor(name, shape, dtype, kind="ExternalOutput").ap()
        return dbg[name]

    ident_d = nc.inline_tensor(np.eye(128, dtype=BF), "ident_inl").ap()
    h128_d = nc.inline_tensor(hadamard128(), "h128_inl").ap()

    sc = Sched()
    es = contextlib.ExitStack()

    def sem(name):
        return Sem(es.enter_context(nc.semaphore(name)))

    with es:
        def sbL(name, shape, dtype):
            return es.enter_context(nc.sbuf_tensor(name, shape, dtype, side="left"))

        ident = sbL("identS", [128, 128], BF16)
        h128 = sbL("h128S", [128, 128], BF16)
        masks = sbL("masksS", [128, 2 * NB * 2 * 128], BF16)
        c_eps = sbL("c_eps", [128, 1], F32)
        c_zero = sbL("c_zero", [128, 1], F32)
        c_mh = sbL("c_mh", [128, 1], F32)
        c_m1 = sbL("c_m1", [128, 1], F32)
        c_scl = sbL("c_scl", [128, 1], F32)
        c_sh = sbL("c_sh", [128, 1], F32)
        onesb = sbL("onesb", [128, 64], F32)
        bn6 = sbL("bn6", [128, 24], F32)
        bn2 = sbL("bn2", [128, 2], F32)
        lnt = sbL("lnt", [128, 1], F32)
        rstd = sbL("rstd", [128, 1], F32)
        bnAll = sbL("bnAll", [128, NB * 24], F32)
        bn2b = sbL("bn2b", [128, 2], F32)
        lntb = sbL("lntb", [128, 1], F32)
        rstdb = sbL("rstdb", [128, 1], F32)

        s_const = sem("s_const"); s_cv = sem("s_cv"); s_bns = sem("s_bns")
        s_xs = [sem("s_xa"), sem("s_xb")]
        s_ln = sem("s_ln"); s_tr = sem("s_tr"); s_trE = sem("s_trE")
        s_wkv = sem("s_wkv"); s_wq = sem("s_wq")
        s_kv = sem("s_kv"); s_kvE = sem("s_kvE"); s_kvS = sem("s_kvS")
        s_kvl = [sem("s_kvla"), sem("s_kvlb")]
        s_q = sem("s_q")
        s_scr = sem("s_scr"); s_p = sem("s_p"); s_pm = sem("s_pm"); s_pF = sem("s_pF")
        s_o = sem("s_o"); s_on = sem("s_on"); s_onD = sem("s_onD"); s_dn = sem("s_dn")
        s_wo = sem("s_wo"); s_op = sem("s_op"); s_b = sem("s_b")
        s_castS = [sem("s_casta"), sem("s_castb")]
        s_cstS = [sem("s_csta"), sem("s_cstb")]
        s_rw = [sem("s_rwa"), sem("s_rwb")]
        s_w2l = sem("s_w2l"); s_w2 = sem("s_w2"); s_w2s = sem("s_w2s")
        s_w2tS = [sem("s_w2ta"), sem("s_w2tb"), sem("s_w2tc"), sem("s_w2td")]; s_w2g = [sem("s_w2ga"), sem("s_w2gb"), sem("s_w2gc"), sem("s_w2gd")]
        s_w2r = [sem("s_w2ra"), sem("s_w2rb"), sem("s_w2rc")]
        s_fab = [sem("s_faba"), sem("s_fabb")]
        s_fw = sem("s_fw"); s_fw2 = sem("s_fw2"); s_fx = sem("s_fx"); s_fp = sem("s_fp"); s_fe = sem("s_fe"); s_wu = sem("s_wu")
        s_sil = sem("s_sil")
        s_x2l = sem("s_x2l"); s_x2s = sem("s_x2s"); s_xnt = sem("s_xnt")
        s_fstl = sem("s_fstl"); s_ffl = sem("s_ffl")
        s_fh = sem("s_fh"); s_y = sem("s_y"); s_dbg = sem("s_dbg")
        s_otv = sem("s_otv"); s_otd = sem("s_otd")

        sc.add("sync", lambda e: e.dma_start(out=ident[:], in_=ident_d[:]), incs=[(s_const, 16)])
        sc.add("sync", lambda e: e.dma_start(out=h128[:], in_=h128_d[:]), incs=[(s_const, 16)])
        sc.add("sync", lambda e: e.dma_start(
            out=masks[:].rearrange("p (a b) -> p a b", b=128),
            in_=masks_d.rearrange("(a p) b -> p a b", p=128)), incs=[(s_const, 16)])
        for buf, v in ((c_eps, EPS), (c_zero, 0.0), (c_mh, -0.5), (c_m1, -1.0),
                       (c_scl, 0.125), (c_sh, EXP_SHIFT), (onesb, 1.0)):
            sc.add("vector", lambda e, buf=buf, v=v: e.memset(buf[:], v), incs=[(s_cv, 1)])
        CONST_DMA = 48
        CONST_V = 7

        wstage = [(w, c) for w in range(2) for c in range(64)]
        wstage_i = [0]
        wsrcs = (w1_d, w3_d)
        wdsts = (w1_bf, w3_bf)

        def emit_wstage(n):
            for _ in range(n):
                if wstage_i[0] >= len(wstage):
                    return
                wi, chnk = wstage[wstage_i[0]]
                half = wstage_i[0] % 2
                ss = s_castS[half]
                st = s_cstS[half]
                sc.add("gpsimd", lambda e, wi=wi, chnk=chnk, half=half: e.dma_start(
                    out=wst[:, 2048 * half:2048 * (half + 1)],
                    in_=wsrcs[wi][128 * chnk:128 * (chnk + 1), :]),
                    waits=sc.guard(("wst", half)), incs=[(ss, 16)])
                sc.add("sync", lambda e, wi=wi, chnk=chnk, half=half: e.dma_start(
                    out=wdsts[wi][128 * chnk:128 * (chnk + 1), :],
                    in_=wst[:, 2048 * half:2048 * (half + 1)]),
                    waits=[(ss, ss.val)], incs=[(st, 16)])
                sc.set_guard(("wst", half), st)
                wstage_i[0] += 1

        with contextlib.ExitStack() as right:
            def sbR(stack, name, shape, dtype):
                return stack.enter_context(nc.sbuf_tensor(name, shape, dtype, side="right"))

            qo_scope = right.enter_context(contextlib.ExitStack())
            q_sb = sbR(qo_scope, "q_sb", [128, 16 * 1024], BF16)
            onorm = sbR(qo_scope, "onorm", [128, 16 * 1024], BF16)
            wst = sbR(qo_scope, "wst", [128, 2 * 2048], BF16)
            dtmp = sbR(qo_scope, "dtmp", [128, 512], F32)

            with contextlib.ExitStack() as pro:
                wk_sb = sbR(pro, "wk_sb", [128, 16 * 512], BF16)
                wv_sb = sbR(pro, "wv_sb", [128, 16 * 512], BF16)
                xbufs = [sbR(pro, f"xbuf{i}", [128, D], F32) for i in range(2)]
                xnb = sbR(pro, "xnb", [128, D], BF16)
                xnT = sbR(pro, "xnT", [128, 16 * 512], BF16)

                sc.add("gpsimd", lambda e: e.dma_start(
                    out=wk_sb[:].rearrange("p (k n) -> p k n", n=512),
                    in_=wk_d.rearrange("(k p) n -> p k n", p=128)), incs=[(s_wkv, 16)])
                sc.add("gpsimd", lambda e: e.dma_start(
                    out=wv_sb[:].rearrange("p (k n) -> p k n", n=512),
                    in_=wv_d.rearrange("(k p) n -> p k n", p=128)), incs=[(s_wkv, 16)])

                def emit_ln_transpose(xb, t_off, trps, trkey, xwait=None, xbkey=None, extra_first=()):
                    for k in range(4):
                        w = ([xwait] if xwait else []) if k == 0 else []
                        sc.add("vector", lambda e, k=k, xb=xb: e.bn_stats(
                            bn6[:, 6 * k:6 * (k + 1)], xb[:, 512 * k:512 * (k + 1)]), waits=w,
                            incs=[(s_bns, 1)] if k == 3 else [])
                    sc.add("vector", lambda e: e.bn_aggr(bn2[:], bn6[:]),
                           waits=[(s_bns, s_bns.val)], incs=[(s_ln, 1)])
                    sc.add("scalar", lambda e: e.activation(out=lnt[:], in_=bn2[:, 1:2], func=AF.Ln, bias=c_eps[:], scale=1.0),
                           waits=[(s_ln, s_ln.val), (s_cv, CONST_V)], incs=[(s_ln, 1)])
                    sc.add("scalar", lambda e: e.activation(out=rstd[:], in_=lnt[:], func=AF.Exp, bias=c_zero[:], scale=c_mh[:]),
                           waits=[(s_ln, s_ln.val)], incs=[(s_ln, 1)])
                    sc.add("vector", lambda e, xb=xb: e.tensor_scalar(
                        xnb[:], xb[:], bn2[:, 0:1], rstd[:], ALU.subtract, ALU.mult),
                        waits=[(s_ln, s_ln.val)], incs=[(s_ln, 1)])
                    vap = s_ln.val
                    if xbkey is not None:
                        sc.guards[xbkey] = (s_ln, vap)
                    for k in range(16):
                        w = (list(extra_first) + [(s_ln, vap)] + sc.guard(trkey)) if k == 0 else []
                        sc.add("tensor", lambda e, k=k, trps=trps: e.transpose(
                            trps[:, 128 * k:128 * (k + 1)], xnb[:, 128 * k:128 * (k + 1)], ident[:]),
                            waits=w, incs=[(s_tr, 1)] if k == 15 else [])
                    sc.add("vector", lambda e, t_off=t_off, trps=trps: e.tensor_copy(
                        xnT[:].rearrange("p (k t) -> p k t", t=512)[:, :, t_off:t_off + 128],
                        trps[:].rearrange("p (k t) -> p k t", t=128)),
                        waits=[(s_tr, s_tr.val)], incs=[(s_trE, 1)])
                    sc.set_guard(trkey, s_trE)

                def load_x(src, row, xb, slot, extra=()):
                    ss = s_xs[slot]
                    sc.add("sync", lambda e, src=src, row=row, xb=xb: e.dma_start(
                        out=xb[:], in_=src[row:row + 128, :]),
                        waits=list(extra) + sc.guard(("xb", slot)), incs=[(ss, 16)])
                    return (ss, ss.val)

                with contextlib.ExitStack() as pscope:
                    trps = [pscope.enter_context(nc.psum_tensor(f"trq{i}", [128, 2048], BF16)) for i in range(2)]
                    mps = [pscope.enter_context(nc.psum_tensor(f"mps{i}", [128, 512], F32)) for i in range(2)]

                    with contextlib.ExitStack() as a0s:
                        wq_sb = sbR(a0s, "wq_sb", [128, 16 * 1024], BF16)
                        for wqh in range(2):
                            for g in range(2):
                                for u in range(2):
                                    for i in range(4):
                                        off = 512 * g + 128 * i + 64 * u
                                        cols = 512 * (2 * wqh + g) + 256 * u + 64 * i
                                        sc.add("gpsimd", lambda e, off=off, cols=cols: e.dma_start(
                                            out=wq_sb[:].rearrange("p (k r) -> p k r", r=1024)[:, :, off:off + 64],
                                            in_=wq_d[:, cols:cols + 64].rearrange(
                                                "(k p) x -> p k x", p=128)),
                                            waits=(sc.guard("wqbuf") if (g == 0 and u == 0 and i == 0) else []),
                                            incs=[(s_wq, 16)])
                            WQ_R = s_wq.val
                            for ch in range(2):
                                for tt in range(4):
                                    lt = 4 * ch + tt
                                    slot = tt % 2
                                    xb = xbufs[slot]
                                    extra = [(s_const, CONST_DMA)] if (wqh == 0 and lt == 0) else []
                                    xw = load_x(x_own, 128 * lt, xb, slot, extra=extra)
                                    emit_ln_transpose(xb, 128 * tt, trps[slot], ("trp", slot), xwait=xw, xbkey=("xb", slot))
                                trE_chunk = s_trE.val
                                for ptl in range(8):
                                    pt = 8 * wqh + ptl
                                    qp = mps[ptl % 2]
                                    for k in range(16):
                                        w = []
                                        if k == 0:
                                            w = [(s_wq, WQ_R), (s_trE, trE_chunk)] + sc.guard(("mps", ptl % 2))
                                        sc.add("tensor", lambda e, ptl=ptl, k=k, qp=qp: e.matmul(
                                            qp[:], wq_sb[:, 1024 * k + 128 * ptl:1024 * k + 128 * (ptl + 1)],
                                            xnT[:, 512 * k:512 * (k + 1)],
                                            start=(k == 0), stop=(k == 15)),
                                            waits=w, incs=[(s_scr, 1)] if k == 15 else [])
                                    sc.add("vector", lambda e, pt=pt, ch=ch, qp=qp: e.tensor_copy(
                                        q_sb[:, 1024 * pt + 512 * ch:1024 * pt + 512 * (ch + 1)], qp[:]),
                                        waits=[(s_scr, s_scr.val)], incs=[(s_q, 1)])
                                    sc.set_guard(("mps", ptl % 2), s_q)
                            sc.set_guard("wqbuf", s_q)
                    Q_DONE = s_q.val
                    if "q" in debug:
                        qd = dbg_out("q_dbg", [128, 16 * 1024], BF16)
                        sc.add("sync", lambda e: e.dma_start(out=qd[:], in_=q_sb[:]),
                               waits=[(s_q, Q_DONE)], incs=[(s_dbg, 16)])

                    kTo = sbR(pro, "kTo", [128, 4 * 1024], BF16)
                    vho = sbR(pro, "vho", [128, 8 * 520], BF16)
                    for eth in range(NB):
                        emit_wstage(12)
                        for tc in range(2):
                            for tt in range(4):
                                gt = 8 * eth + 4 * tc + tt
                                slot = tt % 2
                                xw = load_x(x_full, 128 * gt, xbufs[slot], slot)
                                emit_ln_transpose(xbufs[slot], 128 * tt, trps[slot], ("trp", slot), xwait=xw, xbkey=("xb", slot))
                            trE_chunk = s_trE.val
                            for sub in range(4):
                                kp = mps[sub % 2]
                                for k in range(16):
                                    w = []
                                    if k == 0:
                                        w = [(s_trE, trE_chunk), (s_wkv, 32)] + sc.guard(("mps", sub % 2))
                                    sc.add("tensor", lambda e, sub=sub, k=k, kp=kp: e.matmul(
                                        kp[:], wk_sb[:, 512 * k + 128 * sub:512 * k + 128 * (sub + 1)],
                                        xnT[:, 512 * k:512 * (k + 1)],
                                        start=(k == 0), stop=(k == 15)),
                                        waits=w, incs=[(s_kv, 1)] if k == 15 else [])
                                sc.add("vector", lambda e, sub=sub, tc=tc, kp=kp: e.tensor_copy(
                                    kTo[:, 1024 * sub + 512 * tc:1024 * sub + 512 * (tc + 1)], kp[:]),
                                    waits=[(s_kv, s_kv.val)] + sc.guard(("kvo", 0)), incs=[(s_kvE, 1)])
                                sc.set_guard(("mps", sub % 2), s_kvE)
                            for tt in range(4):
                                vt = 4 * tc + tt
                                vp = mps[tt % 2]
                                for k in range(16):
                                    w = sc.guard(("mps", tt % 2)) if k == 0 else []
                                    sc.add("tensor", lambda e, tt=tt, k=k, vp=vp: e.matmul(
                                        vp[:], xnT[:, 512 * k + 128 * tt:512 * k + 128 * (tt + 1)],
                                        wv_sb[:, 512 * k:512 * (k + 1)],
                                        start=(k == 0), stop=(k == 15)),
                                        waits=w, incs=[(s_kv, 1)] if k == 15 else [])
                                sc.add("vector", lambda e, vt=vt, vp=vp: e.tensor_copy(
                                    vho[:, 520 * vt:520 * vt + 520].rearrange("p (h c) -> p h c", c=65)[:, :, 0:64],
                                    vp[:].rearrange("p (h c) -> p h c", c=64)),
                                    waits=[(s_kv, s_kv.val)] + (sc.guard(("kvo", 0)) if vt == 0 and tc == 0 else []),
                                    incs=[(s_kvE, 1)])
                                sc.set_guard(("mps", tt % 2), s_kvE)
                        sc.add("vector", lambda e: e.memset(
                            vho[:].rearrange("p (t h c) -> p t h c", h=8, c=65)[:, :, :, 64:65], 1.0),
                            incs=[(s_kvE, 1)])
                        KVE = s_kvE.val
                        sc.add("sync", lambda e, eth=eth: e.dma_start(
                            out=kTd[128 * eth:128 * (eth + 1), :], in_=kTo[:]),
                            waits=[(s_kvE, KVE)], incs=[(s_kvS, 16)])
                        sc.add("sync", lambda e, eth=eth: e.dma_start(
                            out=vhd[128 * eth:128 * (eth + 1), :], in_=vho[:]),
                            incs=[(s_kvS, 16)])
                        sc.set_guard(("kvo", 0), s_kvS)
                    KV_STAGED = s_kvS.val

            with contextlib.ExitStack() as attn:
                kTs = [sbR(attn, f"kTs{i}", [128, 4 * 1024], BF16) for i in range(2)]
                vhs = [sbR(attn, f"vhs{i}", [128, 8 * 520], BF16) for i in range(2)]
                Pbuf = sbR(attn, "Pbuf", [128, 2 * 1024], BF16)
                oscr = sbR(attn, "oscr", [128, 512], F32)
                onS = sbR(attn, "onS", [128, 512], BF16)
                wping = sbR(attn, "wping", [128, HFF], BF16)
                wpong = sbR(attn, "wpong", [128, HFF], BF16)

                w2chunk_i = [0]

                def emit_w2prep(n):
                    for _ in range(n):
                        chnk = w2chunk_i[0]
                        if chnk >= 16:
                            return
                        sc.add("gpsimd", lambda e, chnk=chnk: e.dma_start(
                            out=wping[:], in_=w2_d[128 * chnk:128 * (chnk + 1), :]),
                            waits=sc.guard("wping") + ([(s_kvS, KV_STAGED)] if chnk == 0 else []),
                            incs=[(s_w2l, 16)])
                        srcb, dstb = wping, wpong
                        first = [(s_w2l, s_w2l.val)]
                        for stg in range(6):
                            hs = 128 << stg
                            wadd = (first + sc.guard("wpong")) if stg == 0 else [(s_w2, s_w2.val)]
                            sc.add("vector", lambda e, srcb=srcb, dstb=dstb, hs=hs: e.tensor_add(
                                dstb[:].rearrange("p (b two h) -> p b two h", two=2, h=hs)[:, :, 0, :],
                                srcb[:].rearrange("p (b two h) -> p b two h", two=2, h=hs)[:, :, 0, :],
                                srcb[:].rearrange("p (b two h) -> p b two h", two=2, h=hs)[:, :, 1, :]),
                                waits=wadd, incs=[(s_w2, 1)])
                            sc.add("vector", lambda e, srcb=srcb, dstb=dstb, hs=hs: e.tensor_sub(
                                dstb[:].rearrange("p (b two h) -> p b two h", two=2, h=hs)[:, :, 1, :],
                                srcb[:].rearrange("p (b two h) -> p b two h", two=2, h=hs)[:, :, 0, :],
                                srcb[:].rearrange("p (b two h) -> p b two h", two=2, h=hs)[:, :, 1, :]),
                                incs=[(s_w2, 1)])
                            srcb, dstb = dstb, srcb
                        sc.add("gpsimd", lambda e, chnk=chnk, srcb=srcb: e.dma_start(
                            out=w2_bf[128 * chnk:128 * (chnk + 1), :], in_=srcb[:]),
                            waits=[(s_w2, s_w2.val)], incs=[(s_w2s, 16)])
                        sc.guards["wping"] = (s_w2s, s_w2s.val)
                        sc.guards["wpong"] = (s_w2, s_w2.val)
                        w2chunk_i[0] += 1

                kv_ring = {}
                pend = [None]

                if "ot" in debug:
                    ot_sb = sbR(attn, "ot_sb", [65, 512], F32)
                    ot_dbg = dbg_out("ot_dbg", [4 * 65, 512])

                for chunk in (0, 1):
                    b0 = 4 * chunk
                    mlist = list(range(4)) if chunk == 0 else list(range(8))
                    for grp in range(8):
                        emit_w2prep(1)
                        pairs = (2 * grp, 2 * grp + 1)
                        with contextlib.ExitStack() as gsc:
                            opsum = [gsc.enter_context(nc.psum_tensor(f"o{chunk}_{grp}_{i}", [65, 512], F32)) for i in range(4)]
                            with contextlib.ExitStack() as ssc:
                                scp = [ssc.enter_context(nc.psum_tensor(f"s{chunk}_{grp}_{i}", [128, 1024], F32)) for i in range(2)]
                                for mi, m in enumerate(mlist):
                                    slot = m % 2
                                    skl = s_kvl[slot]
                                    sc.add("sync", lambda e, m=m, slot=slot: e.dma_start(
                                        out=kTs[slot][:], in_=kTd[128 * m:128 * (m + 1), :]),
                                        waits=([(s_kvS, 32 * (m + 1))] if (chunk == 0 and grp == 0) else []) + sc.guard(("kvs", slot)),
                                        incs=[(skl, 16)])
                                    sc.add("sync", lambda e, m=m, slot=slot: e.dma_start(
                                        out=vhs[slot][:], in_=vhd[128 * m:128 * (m + 1), :]),
                                        incs=[(skl, 16)])
                                    KVL = (skl, skl.val)
                                    bs = max(m, b0)
                                    noff = (bs - b0) * 128
                                    nw = 512 - noff
                                    tail_here = (b0 <= m <= b0 + 3)
                                    tailcol = (m - b0) * 128 if tail_here else None
                                    for pi, p in enumerate(pairs):
                                        g = p // 4
                                        oA = opsum[2 * pi]
                                        oB = opsum[2 * pi + 1]
                                        for j in range(8):
                                            sp = scp[j % 2]
                                            w = sc.guard(("scp", j % 2)) + [KVL]
                                            if chunk == 0 and grp == 0 and mi == 0 and pi == 0 and j == 0:
                                                w += [(s_q, Q_DONE)]
                                            if mi == 0 and pi == 0 and j == 0:
                                                w += sc.guard("grp_done")
                                            sc.add("tensor", lambda e, g=g, j=j, sp=sp, noff=noff, nw=nw, p=p, chunk=chunk, slot=slot: e.matmul(
                                                sp[:, noff:noff + nw],
                                                kTs[slot][0:64, 1024 * g + 128 * j:1024 * g + 128 * (j + 1)],
                                                q_sb[0:64, 1024 * p + 512 * chunk + noff:1024 * p + 512 * chunk + noff + nw],
                                                start=True, stop=True, tile_position=(0, 0)),
                                                waits=w)
                                            sc.add("tensor", lambda e, g=g, j=j, sp=sp, noff=noff, nw=nw, p=p, chunk=chunk, slot=slot: e.matmul(
                                                sp[:, 512 + noff:512 + noff + nw],
                                                kTs[slot][64:128, 1024 * g + 128 * j:1024 * g + 128 * (j + 1)],
                                                q_sb[64:128, 1024 * p + 512 * chunk + noff:1024 * p + 512 * chunk + noff + nw],
                                                start=True, stop=True, tile_position=(64, 0)),
                                                incs=[(s_scr, 1)])
                                            if pend[0] is not None:
                                                pend[0]()
                                                pend[0] = None
                                            w = [(s_scr, s_scr.val)] + sc.guard(("P", j % 2))
                                            sc.add("scalar", lambda e, j=j, sp=sp, noff=noff, nw=nw: e.activation(
                                                out=Pbuf[:, 1024 * (j % 2):1024 * (j % 2) + 1024].rearrange("p (s t) -> p s t", s=2)[:, :, noff:noff + nw],
                                                in_=sp[:].rearrange("p (s t) -> p s t", s=2)[:, :, noff:noff + nw],
                                                func=AF.Exp, bias=c_sh[:], scale=c_scl[:]),
                                                waits=w, incs=[(s_p, 1)])
                                            sc.set_guard(("scp", j % 2), s_p)
                                            pv_wait = (s_p, s_p.val)
                                            if tail_here:
                                                mcol = ((m % 2) * NB + j) * 2
                                                sc.add("vector", lambda e, j=j, tailcol=tailcol, mcol=mcol: e.tensor_mul(
                                                    Pbuf[:, 1024 * (j % 2) + tailcol:1024 * (j % 2) + tailcol + 128],
                                                    Pbuf[:, 1024 * (j % 2) + tailcol:1024 * (j % 2) + tailcol + 128],
                                                    masks[:, mcol * 128:(mcol + 1) * 128]),
                                                    waits=[(s_p, s_p.val)], incs=[(s_pm, 1)])
                                                sc.add("vector", lambda e, j=j, tailcol=tailcol, mcol=mcol: e.tensor_mul(
                                                    Pbuf[:, 1024 * (j % 2) + 512 + tailcol:1024 * (j % 2) + 512 + tailcol + 128],
                                                    Pbuf[:, 1024 * (j % 2) + 512 + tailcol:1024 * (j % 2) + 512 + tailcol + 128],
                                                    masks[:, (mcol + 1) * 128:(mcol + 2) * 128]),
                                                    incs=[(s_pm, 1)])
                                                pv_wait = (s_pm, s_pm.val)
                                            kvA, kvB = 2 * g, 2 * g + 1
                                            st0 = (mi == 0 and j == 0)
                                            stL = (mi == len(mlist) - 1 and j == 7)

                                            def mk_pv(j=j, kvA=kvA, kvB=kvB, oA=oA, oB=oB, noff=noff, nw=nw,
                                                      st0=st0, stL=stL, slot=slot, pv_wait=pv_wait):
                                                def emit():
                                                    sc.add("tensor", lambda e: e.matmul(
                                                        oA[:, noff:noff + nw],
                                                        vhs[slot][:, 520 * j:520 * j + 520].rearrange("p (h c) -> p h c", c=65)[:, kvA, 0:65],
                                                        Pbuf[:, 1024 * (j % 2) + noff:1024 * (j % 2) + noff + nw],
                                                        start=st0, stop=stL),
                                                        waits=[pv_wait])
                                                    sc.add("tensor", lambda e: e.matmul(
                                                        oB[:, noff:noff + nw],
                                                        vhs[slot][:, 520 * j:520 * j + 520].rearrange("p (h c) -> p h c", c=65)[:, kvB, 0:65],
                                                        Pbuf[:, 1024 * (j % 2) + 512 + noff:1024 * (j % 2) + 512 + noff + nw],
                                                        start=st0, stop=stL),
                                                        incs=[(s_pF, 1)])
                                                    sc.set_guard(("P", j % 2), s_pF)
                                                    sc.set_guard(("kvs", slot), s_pF)
                                                return emit
                                            pend[0] = mk_pv()
                                    if "ot" in debug and chunk == 0 and grp == 0:
                                        sc.add("vector", lambda e, op0=opsum[0]: e.tensor_copy(ot_sb[:], op0[0:65, :]),
                                               waits=[(s_pF, s_pF.val)] + sc.guard("ot_sb"), incs=[(s_otv, 1)])
                                        sc.add("sync", lambda e, mi=mi: e.dma_start(
                                            out=ot_dbg[65 * mi:65 * (mi + 1), :], in_=ot_sb[:]),
                                            waits=[(s_otv, s_otv.val)], incs=[(s_otd, 16)])
                                        sc.guards["ot_sb"] = (s_otd, s_otd.val)
                                        sc.guards[("scp", 0)] = (s_otv, s_otv.val)
                                if pend[0] is not None:
                                    pend[0]()
                                    pend[0] = None
                                LAST_PF = s_pF.val
                            with contextlib.ExitStack() as bsc:
                                bps = [bsc.enter_context(nc.psum_tensor(f"b{chunk}_{grp}_{i}", [64, 512], F32)) for i in range(2)]
                                for pi, p in enumerate(pairs):
                                    for u in range(2):
                                        op_ = opsum[2 * pi + u]
                                        bp = bps[u]
                                        w = [(s_pF, LAST_PF), (s_dn, s_dn.val), (s_o, s_o.val)]
                                        sc.add("scalar", lambda e, op_=op_: e.activation(
                                            out=dtmp[64:65, :], in_=op_[64:65, :], func=AF.Ln, bias=c_zero[64:65, :], scale=1.0),
                                            waits=w, incs=[(s_dn, 1)])
                                        sc.add("scalar", lambda e: e.activation(
                                            out=dtmp[64:65, :], in_=dtmp[64:65, :], func=AF.Exp, bias=c_zero[64:65, :], scale=c_m1[64:65, :]),
                                            waits=[(s_dn, s_dn.val)], incs=[(s_dn, 1)])
                                        DN = s_dn.val
                                        sc.add("tensor", lambda e, bp=bp: e.matmul(
                                            bp[:], onesb[64:65, :], dtmp[64:65, :],
                                            start=True, stop=True),
                                            waits=[(s_dn, DN)] + sc.guard(("bps", u)), incs=[(s_o, 1)])
                                        BC = s_o.val
                                        sc.add("vector", lambda e, op_=op_: e.tensor_copy(oscr[0:64, :], op_[0:64, :]),
                                               waits=[(s_pF, LAST_PF)] + sc.guard("oscr"), incs=[(s_on, 1)])
                                        EV = s_on.val
                                        col = 1024 * p + 512 * chunk
                                        if u == 0:
                                            sc.add("vector", lambda e, col=col, bp=bp: e.tensor_mul(
                                                onorm[0:64, col:col + 512], oscr[0:64, :], bp[:]),
                                                waits=[(s_on, EV), (s_o, BC)], incs=[(s_on, 1)])
                                            sc.set_guard("oscr", s_on)
                                            sc.set_guard(("bps", u), s_on)
                                        else:
                                            sc.add("vector", lambda e, bp=bp: e.tensor_mul(
                                                onS[0:64, :], oscr[0:64, :], bp[:]),
                                                waits=[(s_on, EV), (s_o, BC)] + sc.guard("onS"), incs=[(s_on, 1)])
                                            sc.set_guard("oscr", s_on)
                                            sc.set_guard(("bps", u), s_on)
                                            sc.add("gpsimd", lambda e, col=col: e.dma_start(
                                                out=onorm[64:128, col:col + 512], in_=onS[0:64, :]),
                                                waits=[(s_on, s_on.val)], incs=[(s_onD, 16)])
                                            sc.set_guard("onS", s_onD)
                                sc.set_guard("grp_done", s_on)
                emit_w2prep(1000)
                ONORM_DONE = s_on.val
                OND_DONE = s_onD.val
                emit_wstage(1000)
                W13_WAITS = [(s_cstS[0], s_cstS[0].val), (s_cstS[1], s_cstS[1].val)]

                ONORM_DBG_WAIT = [(s_dbg, s_dbg.val)] if "q" in debug else []
                if "onorm" in debug:
                    od = dbg_out("onorm_dbg", [128, 16 * 1024], BF16)
                    sc.add("sync", lambda e: e.dma_start(out=od[:], in_=onorm[:]),
                           waits=[(s_on, ONORM_DONE), (s_onD, OND_DONE)] + ONORM_DBG_WAIT, incs=[(s_dbg, 16)])
                    ONORM_DBG_WAIT = [(s_dbg, s_dbg.val)]

            W2_STAGED = s_w2s.val
            with contextlib.ExitStack() as bscope:
                wo_sb = sbR(bscope, "wo_sb", [128, 16 * 1024], BF16)
                rawst = sbR(bscope, "rawst", [128, 2 * 1024], F32)
                xbufsB = [sbR(bscope, f"xbB{i}", [128, D], F32) for i in range(2)]
                x2t = sbR(bscope, "x2t", [128, D], F32)
                xn2b = sbR(bscope, "xn2b", [128, D], BF16)
                xn2st = sbR(bscope, "xn2st", [128, 16 * 128], BF16)
                w2tr = sbR(bscope, "w2tr", [128, 4 * 2048], BF16)

                w2t_i = [0]

                def emit_w2t(n):
                    for _ in range(n):
                        j = w2t_i[0]
                        if j >= 64:
                            return
                        slot = j % 4
                        sw = s_w2tS[j % 4]
                        w = ([(s_w2s, W2_STAGED), (s_on, ONORM_DONE), (s_onD, OND_DONE)] + ONORM_DBG_WAIT
                             if j < 4 else sc.guard(("w2t", slot)))
                        sc.add("scalar", lambda e, j=j, slot=slot: e.dma_start(
                            out=w2tr[:, 2048 * slot:2048 * (slot + 1)],
                            in_=w2_bf[:, 128 * j:128 * (j + 1)], transpose=True),
                            waits=w, incs=[(sw, 16)])
                        sc.add("gpsimd", lambda e, j=j, slot=slot: e.dma_start(
                            out=w2T_bf[128 * j:128 * (j + 1), :],
                            in_=w2tr[:, 2048 * slot:2048 * (slot + 1)]),
                            waits=[(sw, sw.val)], incs=[(s_w2g[slot], 16)])
                        sc.set_guard(("w2t", slot), s_w2g[slot])
                        w2t_i[0] += 1

                bns_tile = {}
                with contextlib.ExitStack() as pscope:
                    opp = [pscope.enter_context(nc.psum_tensor(f"opp{i}", [128, 1024], F32)) for i in range(2)]
                    for wh in range(2):
                        for u in range(2):
                            for g in range(4):
                                sc.add("gpsimd", lambda e, wh=wh, u=u, g=g: e.dma_start(
                                    out=wo_sb[64 * u:64 * (u + 1), 4096 * g:4096 * (g + 1)].rearrange("p (i n) -> p i n", n=1024),
                                    in_=wo_d[256 * (2 * g + u):256 * (2 * g + u) + 256,
                                             1024 * wh:1024 * (wh + 1)].rearrange("(i x) n -> x i n", x=64)),
                                    waits=(sc.guard("wo") if (u == 0 and g == 0) else []) +
                                          ([(s_on, ONORM_DONE), (s_onD, OND_DONE)] if (wh == 0 and u == 0 and g == 0) else []),
                                    incs=[(s_wo, 16)])
                        WO_READY = s_wo.val
                        for tt in range(8):
                            op_ = opp[tt % 2]
                            for p in range(16):
                                for half in range(2):
                                    w = []
                                    if p == 0 and half == 0:
                                        w = [(s_wo, WO_READY), (s_on, ONORM_DONE), (s_onD, OND_DONE)] + sc.guard(("opp", tt % 2))
                                    sc.add("tensor", lambda e, p=p, tt=tt, half=half, op_=op_: e.matmul(
                                        op_[:, 512 * half:512 * (half + 1)],
                                        onorm[:, 1024 * p + 128 * tt:1024 * p + 128 * (tt + 1)],
                                        wo_sb[:, 1024 * p + 512 * half:1024 * p + 512 * (half + 1)],
                                        start=(p == 0), stop=(p == 15)),
                                        waits=w,
                                        incs=[(s_op, 1)] if (p == 15 and half == 1) else [])
                            for k in range(2):
                                sc.add("vector", lambda e, tt=tt, k=k, wh=wh, op_=op_: e.bn_stats(
                                    bnAll[:, 24 * tt + 12 * wh + 6 * k:24 * tt + 12 * wh + 6 * (k + 1)],
                                    op_[:, 512 * k:512 * (k + 1)]),
                                    waits=[(s_op, s_op.val)] if k == 0 else [],
                                    incs=[(s_bns, 1)] if k == 1 else [])
                            bns_tile[tt] = s_bns.val
                            sc.add("vector", lambda e, tt=tt, op_=op_: e.tensor_copy(
                                rawst[:, 1024 * (tt % 2):1024 * (tt % 2) + 1024], op_[:]),
                                waits=sc.guard(("rawst", tt % 2)) +
                                      ([(s_w2s, s_w2s.val)] if (wh == 0 and tt < 2) else []),
                                incs=[(s_b, 1)])
                            sc.set_guard(("opp", tt % 2), s_b)
                            srw = s_rw[tt % 2]
                            sc.add("sync", lambda e, tt=tt, wh=wh: e.dma_start(
                                out=raw_dram[128 * tt:128 * (tt + 1), 1024 * wh:1024 * (wh + 1)],
                                in_=rawst[:, 1024 * (tt % 2):1024 * (tt % 2) + 1024]),
                                waits=[(s_b, s_b.val)], incs=[(srw, 16)])
                            sc.set_guard(("rawst", tt % 2), srw)
                            emit_w2t(4)
                        sc.set_guard("wo", s_op)
                    emit_w2t(99)
                    W2T_WAITS = [(g, g.val) for g in s_w2g]
                    RAW_WAITS = [(s_rw[0], s_rw[0].val), (s_rw[1], s_rw[1].val)]

                with contextlib.ExitStack() as pscope:
                    trps = [pscope.enter_context(nc.psum_tensor(f"trc{i}", [128, 2048], BF16)) for i in range(2)]
                    for tt in range(8):
                        slot = tt % 2
                        xb = xbufsB[slot]
                        sxb = s_xs[slot]
                        sc.add("sync", lambda e, tt=tt, xb=xb: e.dma_start(
                            out=xb[:], in_=raw_dram[128 * tt:128 * (tt + 1), :]),
                            waits=RAW_WAITS + sc.guard(("xbB", slot)), incs=[(sxb, 16)])
                        sc.add("sync", lambda e, tt=tt: e.dma_start(
                            out=x2t[:], in_=x_own[128 * tt:128 * (tt + 1), :]),
                            waits=sc.guard("x2t") + sc.guard("x2tS") + (RAW_WAITS if tt == 0 else []),
                            incs=[(s_x2l, 16)])
                        XL = [(sxb, sxb.val), (s_x2l, s_x2l.val)]
                        sc.add("vector", lambda e, tt=tt: e.bn_aggr(bn2b[:], bnAll[:, 24 * tt:24 * (tt + 1)]),
                               waits=[(s_bns, bns_tile[tt])], incs=[(s_ln, 1)])
                        sc.add("scalar", lambda e: e.activation(out=lntb[:], in_=bn2b[:, 1:2], func=AF.Ln, bias=c_eps[:], scale=1.0),
                               waits=[(s_ln, s_ln.val)], incs=[(s_ln, 1)])
                        sc.add("scalar", lambda e: e.activation(out=rstdb[:], in_=lntb[:], func=AF.Exp, bias=c_zero[:], scale=c_mh[:]),
                               waits=[(s_ln, s_ln.val)], incs=[(s_ln, 1)])
                        sc.add("vector", lambda e, xb=xb: e.tensor_scalar(
                            xb[:], xb[:], bn2b[:, 0:1], rstdb[:], ALU.subtract, ALU.mult),
                            waits=[(s_ln, s_ln.val)] + XL, incs=[(s_ln, 1)])
                        sc.guards[("xbB", slot)] = (s_ln, s_ln.val + 1)
                        sc.add("vector", lambda e, xb=xb: e.tensor_add(x2t[:], x2t[:], xb[:]),
                               waits=[(s_ln, s_ln.val)], incs=[(s_ln, 1)])
                        X2R = s_ln.val
                        sc.add("sync", lambda e, tt=tt: e.dma_start(
                            out=x2_dram[128 * tt:128 * (tt + 1), :], in_=x2t[:]),
                            waits=[(s_ln, X2R)], incs=[(s_x2s, 16)])
                        sc.set_guard("x2tS", s_x2s)
                        for k in range(4):
                            sc.add("vector", lambda e, k=k: e.bn_stats(
                                bn6[:, 6 * k:6 * (k + 1)], x2t[:, 512 * k:512 * (k + 1)]),
                                waits=[(s_ln, X2R)] if k == 0 else [],
                                incs=[(s_bns, 1)] if k == 3 else [])
                        sc.add("vector", lambda e: e.bn_aggr(bn2[:], bn6[:]),
                               waits=[(s_bns, s_bns.val)], incs=[(s_ln, 1)])
                        sc.add("scalar", lambda e: e.activation(out=lnt[:], in_=bn2[:, 1:2], func=AF.Ln, bias=c_eps[:], scale=1.0),
                               waits=[(s_ln, s_ln.val)], incs=[(s_ln, 1)])
                        sc.add("scalar", lambda e: e.activation(out=rstd[:], in_=lnt[:], func=AF.Exp, bias=c_zero[:], scale=c_mh[:]),
                               waits=[(s_ln, s_ln.val)], incs=[(s_ln, 1)])
                        sc.add("vector", lambda e: e.tensor_scalar(
                            xn2b[:], x2t[:], bn2[:, 0:1], rstd[:], ALU.subtract, ALU.mult),
                            waits=[(s_ln, s_ln.val)], incs=[(s_ln, 1)])
                        sc.guards["x2t"] = (s_ln, s_ln.val)
                        VAP = s_ln.val
                        tp = trps[slot]
                        for k in range(16):
                            w = []
                            if k == 0:
                                w = [(s_ln, VAP)] + sc.guard(("trc", slot))
                                if tt == 0:
                                    w += [(s_op, s_op.val)]
                            sc.add("tensor", lambda e, k=k, tp=tp: e.transpose(
                                tp[:, 128 * k:128 * (k + 1)], xn2b[:, 128 * k:128 * (k + 1)], ident[:]),
                                waits=w, incs=[(s_tr, 1)] if k == 15 else [])
                        sc.add("vector", lambda e, tp=tp: e.tensor_copy(
                            xn2st[:].rearrange("p (k t) -> p k t", t=128), tp[:].rearrange("p (k t) -> p k t", t=128)),
                            waits=[(s_tr, s_tr.val)] + sc.guard("xn2st"), incs=[(s_b, 1)])
                        sc.set_guard(("trc", slot), s_b)
                        sc.add("sync", lambda e, tt=tt: e.dma_start(
                            out=xn2t_dram.rearrange("(k p) t -> p k t", p=128)[:, :, 128 * tt:128 * (tt + 1)],
                            in_=xn2st[:].rearrange("p (k t) -> p k t", t=128)),
                            waits=[(s_b, s_b.val)], incs=[(s_xnt, 16)])
                        sc.set_guard("xn2st", s_xnt)
                XNT_STORED = s_xnt.val
                X2S_STORED = s_x2s.val

                if "x2" in debug:
                    x2d = dbg_out("x2_dbg", [TOWN, D])
                    sc.add("sync", lambda e: e.dma_start(out=x2d[:], in_=x2_dram[:]),
                           waits=[(s_x2s, X2S_STORED), (s_dbg, s_dbg.val)], incs=[(s_dbg, 16)])

            qo_scope.close()

            with contextlib.ExitStack() as ffn:
                w1tb = [sbR(ffn, f"w1t{i}", [128, 8 * 2048], BF16) for i in range(2)]
                w3tb = [sbR(ffn, f"w3t{i}", [128, 8 * 2048], BF16) for i in range(2)]
                xn2T = sbR(ffn, "xn2T", [128, 16 * 512], BF16)
                hid = sbR(ffn, "hid", [128, 8 * 512], BF16)
                sil = sbR(ffn, "sil", [128, 512], BF16)
                w2s = [sbR(ffn, f"w2s{i}", [128, 1024], BF16) for i in range(3)]
                fas = sbR(ffn, "fas", [128, 2 * 1024], F32)
                fst = sbR(ffn, "fst", [128, 2048], F32)
                ffl = sbR(ffn, "ffl", [128, 2048], F32)

                if True:
                    seqs = [(th, hc) for th in range(2) for hc in range(8)]
                    fw_ready = {}

                    def emit_w13(i):
                        th, hc = seqs[i]
                        buf = i % 2
                        sfw = s_fw if buf == 0 else s_fw2
                        for k in range(16):
                            w = []
                            if k == 0:
                                w = list(W13_WAITS) + sc.guard(("w1t", buf))
                                if i == 0:
                                    w += [(s_xnt, XNT_STORED), (s_x2s, X2S_STORED)] + list(W2T_WAITS) + ONORM_DBG_WAIT
                            sc.add("scalar", lambda e, hc=hc, k=k, buf=buf: e.dma_start(
                                out=w1tb[buf][:, 1024 * k:1024 * (k + 1)],
                                in_=w1_bf[1024 * hc:1024 * (hc + 1), 128 * k:128 * (k + 1)], transpose=True),
                                waits=w, incs=[(sfw, 16)])
                            sc.add("scalar", lambda e, hc=hc, k=k, buf=buf: e.dma_start(
                                out=w3tb[buf][:, 1024 * k:1024 * (k + 1)],
                                in_=w3_bf[1024 * hc:1024 * (hc + 1), 128 * k:128 * (k + 1)], transpose=True),
                                incs=[(sfw, 16)])
                        fw_ready[i] = (sfw, sfw.val)

                    emit_w13(0)
                    XN2_READY = 0
                    for i, (th, hc) in enumerate(seqs):
                        buf = i % 2
                        w1t = w1tb[buf]
                        w3t = w3tb[buf]
                        if hc == 0:
                            sc.add("sync", lambda e, th=th: e.dma_start(
                                out=xn2T[:].rearrange("p (k t) -> p k t", t=512),
                                in_=xn2t_dram.rearrange("(k p) t -> p k t", p=128)[:, :, 512 * th:512 * (th + 1)]),
                                waits=[(s_xnt, XNT_STORED), (s_x2s, X2S_STORED)] + sc.guard("xn2T"), incs=[(s_fx, 16)])
                            XN2_READY = s_fx.val
                        if i + 1 < len(seqs):
                            emit_w13(i + 1)
                        FW_READY = fw_ready[i]
                        upsc = contextlib.ExitStack()
                        gps = [upsc.enter_context(nc.psum_tensor(f"gps{th}_{hc}_{g_}", [128, 512], F32)) for g_ in range(2)]
                        for sub in range(8):
                            gp = gps[sub % 2]
                            for k in range(16):
                                w = []
                                if k == 0:
                                    w = [FW_READY, (s_fx, XN2_READY)] + sc.guard(("gps", sub % 2))
                                    if sub <= 1:
                                        w += sc.guard("gpsbank")
                                    if i == 0 and sub == 0:
                                        w += [(s_b, s_b.val)]
                                sc.add("tensor", lambda e, k=k, sub=sub, gp=gp, w1t=w1t: e.matmul(
                                    gp[:], w1t[:, 1024 * k + 128 * sub:1024 * k + 128 * (sub + 1)],
                                    xn2T[:, 512 * k:512 * (k + 1)],
                                    start=(k == 0), stop=(k == 15)),
                                    waits=w, incs=[(s_fp, 1)] if k == 15 else [])
                            sc.add("scalar", lambda e, gp=gp: e.activation(
                                out=sil[:], in_=gp[:], func=AF.Silu),
                                waits=[(s_fp, s_fp.val)] + sc.guard("sil"), incs=[(s_sil, 1)])
                            SIL = s_sil.val
                            for k in range(16):
                                w = [(s_sil, SIL)] if k == 0 else []
                                sc.add("tensor", lambda e, k=k, sub=sub, gp=gp, w3t=w3t: e.matmul(
                                    gp[:], w3t[:, 1024 * k + 128 * sub:1024 * k + 128 * (sub + 1)],
                                    xn2T[:, 512 * k:512 * (k + 1)],
                                    start=(k == 0), stop=(k == 15)),
                                    waits=w, incs=[(s_fp, 1)] if k == 15 else [])
                            sc.add("vector", lambda e, sub=sub, gp=gp: e.scalar_tensor_tensor(
                                hid[:, 512 * sub:512 * (sub + 1)],
                                gp[:], FWHT_SCALE, sil[:], ALU.mult, ALU.mult),
                                waits=[(s_fp, s_fp.val)], incs=[(s_fh, 1)])
                            sc.set_guard("sil", s_fh)
                            sc.add("tensor", lambda e, sub=sub, gp=gp: e.matmul(
                                gp[:], h128[:], hid[:, 512 * sub:512 * (sub + 1)],
                                start=True, stop=True),
                                waits=[(s_fh, s_fh.val)], incs=[(s_fp, 1)])
                            sc.add("vector", lambda e, sub=sub, gp=gp: e.tensor_copy(
                                hid[:, 512 * sub:512 * (sub + 1)], gp[:]),
                                waits=[(s_fp, s_fp.val)], incs=[(s_fh, 1)])
                            sc.set_guard(("gps", sub % 2), s_fh)
                        HID_READY = s_fh.val
                        upsc.close()
                        sc.set_guard(("w1t", buf), s_fh)
                        for dh in range(2):
                            with contextlib.ExitStack() as wsc:
                                wps4 = [wsc.enter_context(nc.psum_tensor(f"wp{th}_{hc}_{dh}_{g_}", [128, 1024], F32)) for g_ in range(4)]
                                for sub in range(8):
                                    wsl = w2s[sub % 3]
                                    swr = s_w2r[sub % 3]
                                    w = (list(W2T_WAITS) if (i == 0 and dh == 0 and sub == 0) else []) + sc.guard(("w2s", sub % 3))
                                    sc.add("sync", lambda e, hc=hc, dh=dh, sub=sub, wsl=wsl: e.dma_start(
                                        out=wsl[:],
                                        in_=w2T_bf[1024 * hc + 128 * sub:1024 * hc + 128 * (sub + 1), 1024 * dh:1024 * (dh + 1)]),
                                        waits=w, incs=[(swr, 16)])
                                    WND = swr.val
                                    for tt in range(4):
                                        wp = wps4[tt]
                                        for half in range(2):
                                            w = []
                                            if half == 0:
                                                w = [(swr, WND)]
                                                if sub == 0:
                                                    w += [(s_fh, HID_READY)] + sc.guard(("wps", tt))
                                            inc = []
                                            if sub == 7 and half == 1:
                                                inc = [(s_fp, 1)]
                                            elif sub < 7 and tt == 3 and half == 1:
                                                inc = [(s_wu, 1)]
                                            sc.add("tensor", lambda e, sub=sub, tt=tt, half=half, wp=wp, wsl=wsl: e.matmul(
                                                wp[:, 512 * half:512 * (half + 1)],
                                                hid[:, 512 * sub + 128 * tt:512 * sub + 128 * (tt + 1)],
                                                wsl[:, 512 * half:512 * (half + 1)],
                                                start=(sub == 0), stop=(sub == 7)),
                                                waits=w, incs=inc)
                                    if sub == 7:
                                        sc.set_guard(("w2s", sub % 3), s_fp)
                                    else:
                                        sc.set_guard(("w2s", sub % 3), s_wu)
                                for tt in range(4):
                                    gt = 4 * th + tt
                                    fslot = tt % 2
                                    wp = wps4[tt]
                                    sc.add("vector", lambda e, fslot=fslot, wp=wp: e.tensor_copy(
                                        fas[:, 1024 * fslot:1024 * (fslot + 1)], wp[:]),
                                        waits=[(s_fp, s_fp.val)] + sc.guard(("fas", fslot)), incs=[(s_fe, 1)])
                                    sc.set_guard(("wps", tt), s_fe)
                                    sfa = s_fab[fslot]
                                    sc.add("gpsimd", lambda e, gt=gt, dh=dh, fslot=fslot, hc=hc: e.dma_start(
                                        out=facc_dram[128 * gt:128 * (gt + 1), 1024 * dh:1024 * (dh + 1)],
                                        in_=fas[:, 1024 * fslot:1024 * (fslot + 1)],
                                        accum_op=(ALU.add if hc > 0 else ALU.bypass)),
                                        waits=[(s_fe, s_fe.val)], incs=[(sfa, 16)])
                                    sc.set_guard(("fas", fslot), sfa)
                        sc.set_guard("gpsbank", s_fe)
                        if hc == 7:
                            sc.set_guard("xn2T", s_fe)
                    FFN_WAITS = [(s_fab[0], s_fab[0].val), (s_fab[1], s_fab[1].val)]

                    for gt in range(8):
                        sc.add("sync", lambda e, gt=gt: e.dma_start(
                            out=fst[:], in_=x2_dram[128 * gt:128 * (gt + 1), :]),
                            waits=([(s_x2s, X2S_STORED)] if gt == 0 else []) + sc.guard("fst"), incs=[(s_fstl, 16)])
                        sc.add("sync", lambda e, gt=gt: e.dma_start(
                            out=ffl[:], in_=facc_dram[128 * gt:128 * (gt + 1), :]),
                            waits=(list(FFN_WAITS) if gt == 0 else []) + sc.guard("ffl"), incs=[(s_ffl, 16)])
                        XL = [(s_fstl, s_fstl.val), (s_ffl, s_ffl.val)]
                        for k in range(4):
                            sc.add("vector", lambda e, k=k: e.bn_stats(
                                bn6[:, 6 * k:6 * (k + 1)], ffl[:, 512 * k:512 * (k + 1)]),
                                waits=([(s_ffl, s_ffl.val)]) if k == 0 else [],
                                incs=[(s_bns, 1)] if k == 3 else [])
                        sc.add("vector", lambda e: e.bn_aggr(bn2[:], bn6[:]),
                               waits=[(s_bns, s_bns.val)], incs=[(s_ln, 1)])
                        sc.add("scalar", lambda e: e.activation(out=lnt[:], in_=bn2[:, 1:2], func=AF.Ln, bias=c_eps[:], scale=1.0),
                               waits=[(s_ln, s_ln.val)], incs=[(s_ln, 1)])
                        sc.add("scalar", lambda e: e.activation(out=rstd[:], in_=lnt[:], func=AF.Exp, bias=c_zero[:], scale=c_mh[:]),
                               waits=[(s_ln, s_ln.val)], incs=[(s_ln, 1)])
                        sc.add("vector", lambda e: e.tensor_scalar(
                            ffl[:], ffl[:], bn2[:, 0:1], rstd[:], ALU.subtract, ALU.mult),
                            waits=[(s_ln, s_ln.val)], incs=[(s_ln, 1)])
                        sc.guards["ffl"] = (s_ln, s_ln.val + 1)
                        sc.add("vector", lambda e: e.tensor_add(fst[:], fst[:], ffl[:]),
                               waits=[(s_ln, s_ln.val), (s_fstl, s_fstl.val)], incs=[(s_ln, 1)])
                        sc.add("sync", lambda e, gt=gt: e.dma_start(
                            out=y_own[128 * gt:128 * (gt + 1), :], in_=fst[:]),
                            waits=[(s_ln, s_ln.val)], incs=[(s_y, 16)])
                        sc.set_guard("fst", s_y)

                    sc.add("sync", None, waits=[(s_y, s_y.val), (s_dbg, s_dbg.val), (s_otd, s_otd.val)])
                    sc.add("gpsimd", None, waits=[(s_y, s_y.val)])
                    sc.add("tensor", None, waits=[(s_y, s_y.val)])
                    sc.add("scalar", None, waits=[(s_y, s_y.val)])
                    sc.add("vector", None, waits=[(s_y, s_y.val)])

        sc.limit = TRUNC
        global LAST_SEQ
        LAST_SEQ = sc.seq
        with nc.Block() as block:
            block.sync(lambda e: sc.emit("sync", e))
            block.gpsimd(lambda e: sc.emit("gpsimd", e))
            block.tensor(lambda e: sc.emit("tensor", e))
            block.scalar(lambda e: sc.emit("scalar", e))
            block.vector(lambda e: sc.emit("vector", e))

    return nc, dbg


def kernel(**inputs):
    x = np.asarray(inputs["L0_attn_x"])
    wq = np.ascontiguousarray(np.asarray(inputs["L0_attn_Wq"], dtype=np.float32))
    wk = np.ascontiguousarray(np.asarray(inputs["L0_attn_Wk"], dtype=np.float32))
    wv = np.ascontiguousarray(np.asarray(inputs["L0_attn_Wv"], dtype=np.float32))
    wo = np.ascontiguousarray(np.asarray(inputs["L0_attn_Wo"], dtype=np.float32))
    w1 = np.ascontiguousarray(np.asarray(inputs["L0_ffn_W1"], dtype=np.float32))
    w2 = np.ascontiguousarray(np.asarray(inputs["L0_ffn_W2"], dtype=np.float32))
    w3 = np.ascontiguousarray(np.asarray(inputs["L0_ffn_W3"], dtype=np.float32))
    xf = np.ascontiguousarray(np.asarray(x, dtype=np.float32).reshape(S, D))

    nc, dbg = build_program(tuple(DEBUG))

    tril = np.triu(np.ones((128, 128), dtype=np.float32)).astype(BF)
    onesm = np.ones((128, 128), dtype=BF)
    zerom = np.zeros((128, 128), dtype=BF)

    in_maps = []
    for c in range(NCORES):
        idx = np.concatenate([np.arange(128 * g, 128 * (g + 1)) for g in own_tiles(c)])
        mk = np.empty((2, NB, 2, 128, 128), dtype=BF)
        for par in range(2):
            tau = c if par == 0 else 7 - c
            for j in range(NB):
                t = onesm if j < tau else (tril if j == tau else zerom)
                mk[par, j, 0] = t
                mk[par, j, 1] = t
        in_maps.append({
            "x_full": xf, "x_own": np.ascontiguousarray(xf[idx]),
            "Wq": wq, "Wk": wk, "Wv": wv, "Wo": wo,
            "W1": w1, "W2": w2, "W3": w3,
            "masks": np.ascontiguousarray(mk.reshape(2 * NB * 2 * 128, 128)),
        })

    res = run_bass_kernel_spmd(nc, in_maps, core_ids=list(range(NCORES)))
    out = np.empty((S, D), dtype=np.float32)
    for c in range(NCORES):
        idx = np.concatenate([np.arange(128 * g, 128 * (g + 1)) for g in own_tiles(c)])
        out[idx] = res.results[c]["y_own"]
    if DEBUG:
        kernel.last_results = res
    return out.reshape(1, S, D)


# version 08 (final)
# speedup vs baseline: 1.1793x; 1.1793x over previous
import contextlib

import numpy as np
import ml_dtypes

import concourse.bass as bass
import concourse.mybir as mybir
from concourse.bass_utils import run_bass_kernel_spmd

F32 = mybir.dt.float32
BF16 = mybir.dt.bfloat16
AF = mybir.ActivationFunctionType
ALU = mybir.AluOpType
BF = ml_dtypes.bfloat16

D = 2048
S = 8192
NCORES = 8
TOWN = 1024
HFF = 8192
NB = 8
EPS = 1e-5
EXP_SHIFT = -8.0
FWHT_SCALE = float(1.0 / np.sqrt(np.float64(HFF)))

DEBUG = []
TRUNC = None


def own_tiles(c):
    return [8 * b + (c if b % 2 == 0 else 7 - c) for b in range(NB)]


ENGINES = ("sync", "gpsimd", "tensor", "scalar", "vector")


class Sem:
    __slots__ = ("h", "val")

    def __init__(self, handle):
        self.h = handle
        self.val = 0

    def bump(self, n):
        self.val += n
        return self.val


class Op:
    __slots__ = ("fn", "waits", "incs", "seq")

    def __init__(self, fn, waits, incs, seq):
        self.fn = fn
        self.waits = waits
        self.incs = incs
        self.seq = seq


class Sched:
    def __init__(self):
        self.ops = {e: [] for e in ENGINES}
        self.guards = {}
        self.seq = 0
        self.limit = None

    def add(self, eng, fn, waits=(), incs=()):
        seen = {}
        for s, v in waits:
            if v <= 0:
                continue
            if s.h.num in seen and seen[s.h.num][1] >= v:
                continue
            seen[s.h.num] = (s, v)
        self.ops[eng].append(Op(fn, list(seen.values()), list(incs), self.seq))
        self.seq += 1
        for s, n in incs:
            s.bump(n)

    def guard(self, key):
        g = self.guards.get(key)
        return [g] if g else []

    def set_guard(self, key, sem):
        self.guards[key] = (sem, sem.val)

    def trunc_totals(self):
        totals = {}
        for ops in self.ops.values():
            for op in ops:
                if op.seq < self.limit:
                    for s, n in op.incs:
                        totals[s] = totals.get(s, 0) + n
        return totals

    def emit(self, eng_name, eng):
        for op in self.ops[eng_name]:
            if self.limit is not None and op.seq >= self.limit:
                continue
            for s, v in op.waits:
                eng.wait_ge(s.h, v)
            if op.fn is None:
                continue
            ins = op.fn(eng)
            for s, n in op.incs:
                ins = ins.then_inc(s.h, n)
        if self.limit is not None:
            for s, total in self.trunc_totals().items():
                eng.wait_ge(s.h, total)


def hadamard128():
    h = np.ones((1, 1), dtype=np.float32)
    while h.shape[0] < 128:
        h = np.block([[h, h], [h, -h]])
    return h.astype(BF)


def build_program(debug=()):
    nc = bass.Bass("TRN2", target_bir_lowering=False)

    x_full = nc.dram_tensor("x_full", [S, D], F32, kind="ExternalInput").ap()
    x_own = nc.dram_tensor("x_own", [TOWN, D], F32, kind="ExternalInput").ap()
    wq_d = nc.dram_tensor("Wq", [D, D], F32, kind="ExternalInput").ap()
    wk_d = nc.dram_tensor("Wk", [D, 512], F32, kind="ExternalInput").ap()
    wv_d = nc.dram_tensor("Wv", [D, 512], F32, kind="ExternalInput").ap()
    wo_d = nc.dram_tensor("Wo", [D, D], F32, kind="ExternalInput").ap()
    w1_d = nc.dram_tensor("W1", [HFF, D], F32, kind="ExternalInput").ap()
    w2_d = nc.dram_tensor("W2", [D, HFF], F32, kind="ExternalInput").ap()
    w3_d = nc.dram_tensor("W3", [HFF, D], F32, kind="ExternalInput").ap()
    masks_d = nc.dram_tensor("masks", [2 * NB * 2 * 128, 128], BF16, kind="ExternalInput").ap()
    y_own = nc.dram_tensor("y_own", [TOWN, D], F32, kind="ExternalOutput").ap()

    w1_bf = nc.dram_tensor("w1_bf", [HFF, D], BF16, kind="Internal").ap()
    w3_bf = nc.dram_tensor("w3_bf", [HFF, D], BF16, kind="Internal").ap()
    w2_bf = nc.dram_tensor("w2_bf", [D, HFF], BF16, kind="Internal").ap()
    w2T_bf = nc.dram_tensor("w2T_bf", [HFF, D], BF16, kind="Internal").ap()
    facc_dram = nc.dram_tensor("facc_dram", [TOWN, D], F32, kind="Internal").ap()
    kTd = nc.dram_tensor("kTd", [NB * 128, 4 * 1024], BF16, kind="Internal").ap()
    vhd = nc.dram_tensor("vhd", [NB * 128, 8 * 520], BF16, kind="Internal").ap()
    raw_dram = nc.dram_tensor("raw_dram", [TOWN, D], F32, kind="Internal").ap()
    x2_dram = nc.dram_tensor("x2_dram", [TOWN, D], F32, kind="Internal").ap()
    xn2t_dram = nc.dram_tensor("xn2t_dram", [D, TOWN], BF16, kind="Internal").ap()

    dbg = {}

    def dbg_out(name, shape, dtype=F32):
        dbg[name] = nc.dram_tensor(name, shape, dtype, kind="ExternalOutput").ap()
        return dbg[name]

    ident_d = nc.inline_tensor(np.eye(128, dtype=BF), "ident_inl").ap()
    h128_d = nc.inline_tensor(hadamard128(), "h128_inl").ap()

    sc = Sched()
    es = contextlib.ExitStack()

    def sem(name):
        return Sem(es.enter_context(nc.semaphore(name)))

    with es:
        def sbL(name, shape, dtype):
            return es.enter_context(nc.sbuf_tensor(name, shape, dtype, side="left"))

        ident = sbL("identS", [128, 128], BF16)
        h128 = sbL("h128S", [128, 128], BF16)
        masks = sbL("masksS", [128, 2 * NB * 2 * 128], BF16)
        c_eps = sbL("c_eps", [128, 1], F32)
        c_zero = sbL("c_zero", [128, 1], F32)
        c_mh = sbL("c_mh", [128, 1], F32)
        c_m1 = sbL("c_m1", [128, 1], F32)
        c_scl = sbL("c_scl", [128, 1], F32)
        c_sh = sbL("c_sh", [128, 1], F32)
        onesb = sbL("onesb", [128, 64], F32)
        bn6 = sbL("bn6", [128, 24], F32)
        bn2 = sbL("bn2", [128, 2], F32)
        lnt = sbL("lnt", [128, 1], F32)
        rstd = sbL("rstd", [128, 1], F32)
        bnAll = sbL("bnAll", [128, NB * 24], F32)
        bn2b = sbL("bn2b", [128, 2], F32)
        lntb = sbL("lntb", [128, 1], F32)
        rstdb = sbL("rstdb", [128, 1], F32)

        s_const = sem("s_const"); s_cv = sem("s_cv"); s_bns = sem("s_bns")
        s_xs = [sem("s_xa"), sem("s_xb")]
        s_ln = sem("s_ln"); s_tr = sem("s_tr"); s_trE = sem("s_trE")
        s_wkv = sem("s_wkv"); s_wq = sem("s_wq")
        s_kv = sem("s_kv"); s_kvE = sem("s_kvE"); s_kvS = sem("s_kvS")
        s_kvl = [sem("s_kvla"), sem("s_kvlb")]
        s_q = sem("s_q")
        s_scr = sem("s_scr"); s_p = sem("s_p"); s_pm = sem("s_pm"); s_pF = sem("s_pF")
        s_o = sem("s_o"); s_on = sem("s_on"); s_onD = sem("s_onD"); s_dn = sem("s_dn")
        s_wo = sem("s_wo"); s_op = sem("s_op"); s_b = sem("s_b")
        s_castS = [sem("s_casta"), sem("s_castb")]
        s_cstS = [sem("s_csta"), sem("s_cstb")]
        s_rw = [sem("s_rwa"), sem("s_rwb")]
        s_w2l = sem("s_w2l"); s_w2 = sem("s_w2"); s_w2s = sem("s_w2s")
        s_w2tS = [sem("s_w2ta"), sem("s_w2tb"), sem("s_w2tc"), sem("s_w2td")]; s_w2g = [sem("s_w2ga"), sem("s_w2gb"), sem("s_w2gc"), sem("s_w2gd")]
        s_w2r = [sem("s_w2ra"), sem("s_w2rb"), sem("s_w2rc")]
        s_fab = [sem("s_faba"), sem("s_fabb")]
        s_fw = sem("s_fw"); s_fx = sem("s_fx"); s_fp = sem("s_fp"); s_fe = sem("s_fe"); s_wu = sem("s_wu")
        s_sil = sem("s_sil")
        s_x2l = sem("s_x2l"); s_x2s = sem("s_x2s"); s_xnt = sem("s_xnt")
        s_fstl = sem("s_fstl"); s_ffl = sem("s_ffl"); s_fstl2 = sem("s_fstl2"); s_ffl2 = sem("s_ffl2")
        s_fh = sem("s_fh"); s_y = sem("s_y"); s_y2 = sem("s_y2"); s_dbg = sem("s_dbg")
        s_otv = sem("s_otv"); s_otd = sem("s_otd")

        sc.add("sync", lambda e: e.dma_start(out=ident[:], in_=ident_d[:]), incs=[(s_const, 16)])
        sc.add("sync", lambda e: e.dma_start(out=h128[:], in_=h128_d[:]), incs=[(s_const, 16)])
        sc.add("sync", lambda e: e.dma_start(
            out=masks[:].rearrange("p (a b) -> p a b", b=128),
            in_=masks_d.rearrange("(a p) b -> p a b", p=128)), incs=[(s_const, 16)])
        for buf, v in ((c_eps, EPS), (c_zero, 0.0), (c_mh, -0.5), (c_m1, -1.0),
                       (c_scl, 0.125), (c_sh, EXP_SHIFT), (onesb, 1.0)):
            sc.add("vector", lambda e, buf=buf, v=v: e.memset(buf[:], v), incs=[(s_cv, 1)])
        CONST_DMA = 48
        CONST_V = 7

        wstage = [(w, c) for w in range(2) for c in range(64)]
        wstage_i = [0]
        wsrcs = (w1_d, w3_d)
        wdsts = (w1_bf, w3_bf)

        def emit_wstage(n):
            for _ in range(n):
                if wstage_i[0] >= len(wstage):
                    return
                wi, chnk = wstage[wstage_i[0]]
                half = wstage_i[0] % 2
                ss = s_castS[half]
                st = s_cstS[half]
                sc.add("gpsimd", lambda e, wi=wi, chnk=chnk, half=half: e.dma_start(
                    out=wst[:, 2048 * half:2048 * (half + 1)],
                    in_=wsrcs[wi][128 * chnk:128 * (chnk + 1), :]),
                    waits=sc.guard(("wst", half)), incs=[(ss, 16)])
                sc.add("sync", lambda e, wi=wi, chnk=chnk, half=half: e.dma_start(
                    out=wdsts[wi][128 * chnk:128 * (chnk + 1), :],
                    in_=wst[:, 2048 * half:2048 * (half + 1)]),
                    waits=[(ss, ss.val)], incs=[(st, 16)])
                sc.set_guard(("wst", half), st)
                wstage_i[0] += 1

        with contextlib.ExitStack() as right:
            def sbR(stack, name, shape, dtype):
                return stack.enter_context(nc.sbuf_tensor(name, shape, dtype, side="right"))

            qo_scope = right.enter_context(contextlib.ExitStack())
            q_sb = sbR(qo_scope, "q_sb", [128, 16 * 1024], BF16)
            onorm = sbR(qo_scope, "onorm", [128, 16 * 1024], BF16)
            wst = sbR(qo_scope, "wst", [128, 2 * 2048], BF16)
            dtmp = sbR(qo_scope, "dtmp", [128, 512], F32)

            with contextlib.ExitStack() as pro:
                wk_sb = sbR(pro, "wk_sb", [128, 16 * 512], BF16)
                wv_sb = sbR(pro, "wv_sb", [128, 16 * 512], BF16)
                xbufs = [sbR(pro, f"xbuf{i}", [128, D], F32) for i in range(2)]
                xnb = sbR(pro, "xnb", [128, D], BF16)
                xnT = sbR(pro, "xnT", [128, 16 * 512], BF16)

                sc.add("gpsimd", lambda e: e.dma_start(
                    out=wk_sb[:].rearrange("p (k n) -> p k n", n=512),
                    in_=wk_d.rearrange("(k p) n -> p k n", p=128)), incs=[(s_wkv, 16)])
                sc.add("gpsimd", lambda e: e.dma_start(
                    out=wv_sb[:].rearrange("p (k n) -> p k n", n=512),
                    in_=wv_d.rearrange("(k p) n -> p k n", p=128)), incs=[(s_wkv, 16)])

                def emit_ln_transpose(xb, t_off, trps, trkey, xwait=None, xbkey=None, extra_first=()):
                    for k in range(4):
                        w = ([xwait] if xwait else []) if k == 0 else []
                        sc.add("vector", lambda e, k=k, xb=xb: e.bn_stats(
                            bn6[:, 6 * k:6 * (k + 1)], xb[:, 512 * k:512 * (k + 1)]), waits=w,
                            incs=[(s_bns, 1)] if k == 3 else [])
                    sc.add("vector", lambda e: e.bn_aggr(bn2[:], bn6[:]),
                           waits=[(s_bns, s_bns.val)], incs=[(s_ln, 1)])
                    sc.add("scalar", lambda e: e.activation(out=lnt[:], in_=bn2[:, 1:2], func=AF.Ln, bias=c_eps[:], scale=1.0),
                           waits=[(s_ln, s_ln.val), (s_cv, CONST_V)], incs=[(s_ln, 1)])
                    sc.add("scalar", lambda e: e.activation(out=rstd[:], in_=lnt[:], func=AF.Exp, bias=c_zero[:], scale=c_mh[:]),
                           waits=[(s_ln, s_ln.val)], incs=[(s_ln, 1)])
                    sc.add("vector", lambda e, xb=xb: e.tensor_scalar(
                        xnb[:], xb[:], bn2[:, 0:1], rstd[:], ALU.subtract, ALU.mult),
                        waits=[(s_ln, s_ln.val)], incs=[(s_ln, 1)])
                    vap = s_ln.val
                    if xbkey is not None:
                        sc.guards[xbkey] = (s_ln, vap)
                    for k in range(16):
                        w = (list(extra_first) + [(s_ln, vap)] + sc.guard(trkey)) if k == 0 else []
                        sc.add("tensor", lambda e, k=k, trps=trps: e.transpose(
                            trps[:, 128 * k:128 * (k + 1)], xnb[:, 128 * k:128 * (k + 1)], ident[:]),
                            waits=w, incs=[(s_tr, 1)] if k == 15 else [])
                    sc.add("vector", lambda e, t_off=t_off, trps=trps: e.tensor_copy(
                        xnT[:].rearrange("p (k t) -> p k t", t=512)[:, :, t_off:t_off + 128],
                        trps[:].rearrange("p (k t) -> p k t", t=128)),
                        waits=[(s_tr, s_tr.val)], incs=[(s_trE, 1)])
                    sc.set_guard(trkey, s_trE)

                def load_x(src, row, xb, slot, extra=()):
                    ss = s_xs[slot]
                    sc.add("sync", lambda e, src=src, row=row, xb=xb: e.dma_start(
                        out=xb[:], in_=src[row:row + 128, :]),
                        waits=list(extra) + sc.guard(("xb", slot)), incs=[(ss, 16)])
                    return (ss, ss.val)

                with contextlib.ExitStack() as pscope:
                    trps = [pscope.enter_context(nc.psum_tensor(f"trq{i}", [128, 2048], BF16)) for i in range(2)]
                    mps = [pscope.enter_context(nc.psum_tensor(f"mps{i}", [128, 512], F32)) for i in range(2)]

                    with contextlib.ExitStack() as a0s:
                        wq_sb = sbR(a0s, "wq_sb", [128, 16 * 1024], BF16)
                        for wqh in range(2):
                            for g in range(2):
                                for u in range(2):
                                    for i in range(4):
                                        off = 512 * g + 128 * i + 64 * u
                                        cols = 512 * (2 * wqh + g) + 256 * u + 64 * i
                                        sc.add("gpsimd", lambda e, off=off, cols=cols: e.dma_start(
                                            out=wq_sb[:].rearrange("p (k r) -> p k r", r=1024)[:, :, off:off + 64],
                                            in_=wq_d[:, cols:cols + 64].rearrange(
                                                "(k p) x -> p k x", p=128)),
                                            waits=(sc.guard("wqbuf") if (g == 0 and u == 0 and i == 0) else []),
                                            incs=[(s_wq, 16)])
                            WQ_R = s_wq.val
                            for ch in range(2):
                                for tt in range(4):
                                    lt = 4 * ch + tt
                                    slot = tt % 2
                                    xb = xbufs[slot]
                                    extra = [(s_const, CONST_DMA)] if (wqh == 0 and lt == 0) else []
                                    xw = load_x(x_own, 128 * lt, xb, slot, extra=extra)
                                    emit_ln_transpose(xb, 128 * tt, trps[slot], ("trp", slot), xwait=xw, xbkey=("xb", slot))
                                trE_chunk = s_trE.val
                                for ptl in range(8):
                                    pt = 8 * wqh + ptl
                                    qp = mps[ptl % 2]
                                    for k in range(16):
                                        w = []
                                        if k == 0:
                                            w = [(s_wq, WQ_R), (s_trE, trE_chunk)] + sc.guard(("mps", ptl % 2))
                                        sc.add("tensor", lambda e, ptl=ptl, k=k, qp=qp: e.matmul(
                                            qp[:], wq_sb[:, 1024 * k + 128 * ptl:1024 * k + 128 * (ptl + 1)],
                                            xnT[:, 512 * k:512 * (k + 1)],
                                            start=(k == 0), stop=(k == 15)),
                                            waits=w, incs=[(s_scr, 1)] if k == 15 else [])
                                    sc.add("vector", lambda e, pt=pt, ch=ch, qp=qp: e.tensor_copy(
                                        q_sb[:, 1024 * pt + 512 * ch:1024 * pt + 512 * (ch + 1)], qp[:]),
                                        waits=[(s_scr, s_scr.val)], incs=[(s_q, 1)])
                                    sc.set_guard(("mps", ptl % 2), s_q)
                            sc.set_guard("wqbuf", s_q)
                    Q_DONE = s_q.val
                    if "q" in debug:
                        qd = dbg_out("q_dbg", [128, 16 * 1024], BF16)
                        sc.add("sync", lambda e: e.dma_start(out=qd[:], in_=q_sb[:]),
                               waits=[(s_q, Q_DONE)], incs=[(s_dbg, 16)])

                    kTo = sbR(pro, "kTo", [128, 4 * 1024], BF16)
                    vho = sbR(pro, "vho", [128, 8 * 520], BF16)
                    for eth in range(NB):
                        emit_wstage(12)
                        for tc in range(2):
                            for tt in range(4):
                                gt = 8 * eth + 4 * tc + tt
                                slot = tt % 2
                                xw = load_x(x_full, 128 * gt, xbufs[slot], slot)
                                emit_ln_transpose(xbufs[slot], 128 * tt, trps[slot], ("trp", slot), xwait=xw, xbkey=("xb", slot))
                            trE_chunk = s_trE.val
                            for sub in range(4):
                                kp = mps[sub % 2]
                                for k in range(16):
                                    w = []
                                    if k == 0:
                                        w = [(s_trE, trE_chunk), (s_wkv, 32)] + sc.guard(("mps", sub % 2))
                                    sc.add("tensor", lambda e, sub=sub, k=k, kp=kp: e.matmul(
                                        kp[:], wk_sb[:, 512 * k + 128 * sub:512 * k + 128 * (sub + 1)],
                                        xnT[:, 512 * k:512 * (k + 1)],
                                        start=(k == 0), stop=(k == 15)),
                                        waits=w, incs=[(s_kv, 1)] if k == 15 else [])
                                sc.add("vector", lambda e, sub=sub, tc=tc, kp=kp: e.tensor_copy(
                                    kTo[:, 1024 * sub + 512 * tc:1024 * sub + 512 * (tc + 1)], kp[:]),
                                    waits=[(s_kv, s_kv.val)] + sc.guard(("kvo", 0)), incs=[(s_kvE, 1)])
                                sc.set_guard(("mps", sub % 2), s_kvE)
                            for tt in range(4):
                                vt = 4 * tc + tt
                                vp = mps[tt % 2]
                                for k in range(16):
                                    w = sc.guard(("mps", tt % 2)) if k == 0 else []
                                    sc.add("tensor", lambda e, tt=tt, k=k, vp=vp: e.matmul(
                                        vp[:], xnT[:, 512 * k + 128 * tt:512 * k + 128 * (tt + 1)],
                                        wv_sb[:, 512 * k:512 * (k + 1)],
                                        start=(k == 0), stop=(k == 15)),
                                        waits=w, incs=[(s_kv, 1)] if k == 15 else [])
                                sc.add("vector", lambda e, vt=vt, vp=vp: e.tensor_copy(
                                    vho[:, 520 * vt:520 * vt + 520].rearrange("p (h c) -> p h c", c=65)[:, :, 0:64],
                                    vp[:].rearrange("p (h c) -> p h c", c=64)),
                                    waits=[(s_kv, s_kv.val)] + (sc.guard(("kvo", 0)) if vt == 0 and tc == 0 else []),
                                    incs=[(s_kvE, 1)])
                                sc.set_guard(("mps", tt % 2), s_kvE)
                        sc.add("vector", lambda e: e.memset(
                            vho[:].rearrange("p (t h c) -> p t h c", h=8, c=65)[:, :, :, 64:65], 1.0),
                            incs=[(s_kvE, 1)])
                        KVE = s_kvE.val
                        sc.add("sync", lambda e, eth=eth: e.dma_start(
                            out=kTd[128 * eth:128 * (eth + 1), :], in_=kTo[:]),
                            waits=[(s_kvE, KVE)], incs=[(s_kvS, 16)])
                        sc.add("sync", lambda e, eth=eth: e.dma_start(
                            out=vhd[128 * eth:128 * (eth + 1), :], in_=vho[:]),
                            incs=[(s_kvS, 16)])
                        sc.set_guard(("kvo", 0), s_kvS)
                    KV_STAGED = s_kvS.val

            with contextlib.ExitStack() as attn:
                kTs = [sbR(attn, f"kTs{i}", [128, 4 * 1024], BF16) for i in range(2)]
                vhs = [sbR(attn, f"vhs{i}", [128, 8 * 520], BF16) for i in range(2)]
                Pbuf = sbR(attn, "Pbuf", [128, 2 * 1024], BF16)
                oscr = sbR(attn, "oscr", [128, 512], F32)
                onS = sbR(attn, "onS", [128, 512], BF16)
                wping = sbR(attn, "wping", [128, HFF], BF16)
                wpong = sbR(attn, "wpong", [128, HFF], BF16)

                w2chunk_i = [0]

                def emit_w2prep(n):
                    for _ in range(n):
                        chnk = w2chunk_i[0]
                        if chnk >= 16:
                            return
                        sc.add("gpsimd", lambda e, chnk=chnk: e.dma_start(
                            out=wping[:], in_=w2_d[128 * chnk:128 * (chnk + 1), :]),
                            waits=sc.guard("wping") + ([(s_kvS, KV_STAGED)] if chnk == 0 else []),
                            incs=[(s_w2l, 16)])
                        srcb, dstb = wping, wpong
                        first = [(s_w2l, s_w2l.val)]
                        for stg in range(6):
                            hs = 128 << stg
                            wadd = (first + sc.guard("wpong")) if stg == 0 else [(s_w2, s_w2.val)]
                            sc.add("vector", lambda e, srcb=srcb, dstb=dstb, hs=hs: e.tensor_add(
                                dstb[:].rearrange("p (b two h) -> p b two h", two=2, h=hs)[:, :, 0, :],
                                srcb[:].rearrange("p (b two h) -> p b two h", two=2, h=hs)[:, :, 0, :],
                                srcb[:].rearrange("p (b two h) -> p b two h", two=2, h=hs)[:, :, 1, :]),
                                waits=wadd, incs=[(s_w2, 1)])
                            sc.add("vector", lambda e, srcb=srcb, dstb=dstb, hs=hs: e.tensor_sub(
                                dstb[:].rearrange("p (b two h) -> p b two h", two=2, h=hs)[:, :, 1, :],
                                srcb[:].rearrange("p (b two h) -> p b two h", two=2, h=hs)[:, :, 0, :],
                                srcb[:].rearrange("p (b two h) -> p b two h", two=2, h=hs)[:, :, 1, :]),
                                incs=[(s_w2, 1)])
                            srcb, dstb = dstb, srcb
                        sc.add("gpsimd", lambda e, chnk=chnk, srcb=srcb: e.dma_start(
                            out=w2_bf[128 * chnk:128 * (chnk + 1), :], in_=srcb[:]),
                            waits=[(s_w2, s_w2.val)], incs=[(s_w2s, 16)])
                        sc.guards["wping"] = (s_w2s, s_w2s.val)
                        sc.guards["wpong"] = (s_w2, s_w2.val)
                        w2chunk_i[0] += 1

                kv_ring = {}
                pend = [None]

                if "ot" in debug:
                    ot_sb = sbR(attn, "ot_sb", [65, 512], F32)
                    ot_dbg = dbg_out("ot_dbg", [4 * 65, 512])

                for chunk in (0, 1):
                    b0 = 4 * chunk
                    mlist = list(range(4)) if chunk == 0 else list(range(8))
                    for grp in range(8):
                        emit_w2prep(1)
                        pairs = (2 * grp, 2 * grp + 1)
                        with contextlib.ExitStack() as gsc:
                            opsum = [gsc.enter_context(nc.psum_tensor(f"o{chunk}_{grp}_{i}", [65, 512], F32)) for i in range(4)]
                            with contextlib.ExitStack() as ssc:
                                scp = [ssc.enter_context(nc.psum_tensor(f"s{chunk}_{grp}_{i}", [128, 1024], F32)) for i in range(2)]
                                for mi, m in enumerate(mlist):
                                    slot = m % 2
                                    skl = s_kvl[slot]
                                    sc.add("sync", lambda e, m=m, slot=slot: e.dma_start(
                                        out=kTs[slot][:], in_=kTd[128 * m:128 * (m + 1), :]),
                                        waits=([(s_kvS, 32 * (m + 1))] if (chunk == 0 and grp == 0) else []) + sc.guard(("kvs", slot)),
                                        incs=[(skl, 16)])
                                    sc.add("sync", lambda e, m=m, slot=slot: e.dma_start(
                                        out=vhs[slot][:], in_=vhd[128 * m:128 * (m + 1), :]),
                                        incs=[(skl, 16)])
                                    KVL = (skl, skl.val)
                                    bs = max(m, b0)
                                    noff = (bs - b0) * 128
                                    nw = 512 - noff
                                    tail_here = (b0 <= m <= b0 + 3)
                                    tailcol = (m - b0) * 128 if tail_here else None
                                    for pi, p in enumerate(pairs):
                                        g = p // 4
                                        oA = opsum[2 * pi]
                                        oB = opsum[2 * pi + 1]
                                        for j in range(8):
                                            sp = scp[j % 2]
                                            w = sc.guard(("scp", j % 2)) + [KVL]
                                            if chunk == 0 and grp == 0 and mi == 0 and pi == 0 and j == 0:
                                                w += [(s_q, Q_DONE)]
                                            if mi == 0 and pi == 0 and j == 0:
                                                w += sc.guard("grp_done")
                                            sc.add("tensor", lambda e, g=g, j=j, sp=sp, noff=noff, nw=nw, p=p, chunk=chunk, slot=slot: e.matmul(
                                                sp[:, noff:noff + nw],
                                                kTs[slot][0:64, 1024 * g + 128 * j:1024 * g + 128 * (j + 1)],
                                                q_sb[0:64, 1024 * p + 512 * chunk + noff:1024 * p + 512 * chunk + noff + nw],
                                                start=True, stop=True, tile_position=(0, 0)),
                                                waits=w)
                                            sc.add("tensor", lambda e, g=g, j=j, sp=sp, noff=noff, nw=nw, p=p, chunk=chunk, slot=slot: e.matmul(
                                                sp[:, 512 + noff:512 + noff + nw],
                                                kTs[slot][64:128, 1024 * g + 128 * j:1024 * g + 128 * (j + 1)],
                                                q_sb[64:128, 1024 * p + 512 * chunk + noff:1024 * p + 512 * chunk + noff + nw],
                                                start=True, stop=True, tile_position=(64, 0)),
                                                incs=[(s_scr, 1)])
                                            if pend[0] is not None:
                                                pend[0]()
                                                pend[0] = None
                                            w = [(s_scr, s_scr.val)] + sc.guard(("P", j % 2))
                                            sc.add("scalar", lambda e, j=j, sp=sp, noff=noff, nw=nw: e.activation(
                                                out=Pbuf[:, 1024 * (j % 2):1024 * (j % 2) + 1024].rearrange("p (s t) -> p s t", s=2)[:, :, noff:noff + nw],
                                                in_=sp[:].rearrange("p (s t) -> p s t", s=2)[:, :, noff:noff + nw],
                                                func=AF.Exp, bias=c_sh[:], scale=c_scl[:]),
                                                waits=w, incs=[(s_p, 1)])
                                            sc.set_guard(("scp", j % 2), s_p)
                                            pv_wait = (s_p, s_p.val)
                                            if tail_here:
                                                mcol = ((m % 2) * NB + j) * 2
                                                sc.add("vector", lambda e, j=j, tailcol=tailcol, mcol=mcol: e.tensor_mul(
                                                    Pbuf[:, 1024 * (j % 2) + tailcol:1024 * (j % 2) + tailcol + 128],
                                                    Pbuf[:, 1024 * (j % 2) + tailcol:1024 * (j % 2) + tailcol + 128],
                                                    masks[:, mcol * 128:(mcol + 1) * 128]),
                                                    waits=[(s_p, s_p.val)], incs=[(s_pm, 1)])
                                                sc.add("vector", lambda e, j=j, tailcol=tailcol, mcol=mcol: e.tensor_mul(
                                                    Pbuf[:, 1024 * (j % 2) + 512 + tailcol:1024 * (j % 2) + 512 + tailcol + 128],
                                                    Pbuf[:, 1024 * (j % 2) + 512 + tailcol:1024 * (j % 2) + 512 + tailcol + 128],
                                                    masks[:, (mcol + 1) * 128:(mcol + 2) * 128]),
                                                    incs=[(s_pm, 1)])
                                                pv_wait = (s_pm, s_pm.val)
                                            kvA, kvB = 2 * g, 2 * g + 1
                                            st0 = (mi == 0 and j == 0)
                                            stL = (mi == len(mlist) - 1 and j == 7)

                                            def mk_pv(j=j, kvA=kvA, kvB=kvB, oA=oA, oB=oB, noff=noff, nw=nw,
                                                      st0=st0, stL=stL, slot=slot, pv_wait=pv_wait):
                                                def emit():
                                                    sc.add("tensor", lambda e: e.matmul(
                                                        oA[:, noff:noff + nw],
                                                        vhs[slot][:, 520 * j:520 * j + 520].rearrange("p (h c) -> p h c", c=65)[:, kvA, 0:65],
                                                        Pbuf[:, 1024 * (j % 2) + noff:1024 * (j % 2) + noff + nw],
                                                        start=st0, stop=stL),
                                                        waits=[pv_wait])
                                                    sc.add("tensor", lambda e: e.matmul(
                                                        oB[:, noff:noff + nw],
                                                        vhs[slot][:, 520 * j:520 * j + 520].rearrange("p (h c) -> p h c", c=65)[:, kvB, 0:65],
                                                        Pbuf[:, 1024 * (j % 2) + 512 + noff:1024 * (j % 2) + 512 + noff + nw],
                                                        start=st0, stop=stL),
                                                        incs=[(s_pF, 1)])
                                                    sc.set_guard(("P", j % 2), s_pF)
                                                    sc.set_guard(("kvs", slot), s_pF)
                                                return emit
                                            pend[0] = mk_pv()
                                    if "ot" in debug and chunk == 0 and grp == 0:
                                        sc.add("vector", lambda e, op0=opsum[0]: e.tensor_copy(ot_sb[:], op0[0:65, :]),
                                               waits=[(s_pF, s_pF.val)] + sc.guard("ot_sb"), incs=[(s_otv, 1)])
                                        sc.add("sync", lambda e, mi=mi: e.dma_start(
                                            out=ot_dbg[65 * mi:65 * (mi + 1), :], in_=ot_sb[:]),
                                            waits=[(s_otv, s_otv.val)], incs=[(s_otd, 16)])
                                        sc.guards["ot_sb"] = (s_otd, s_otd.val)
                                        sc.guards[("scp", 0)] = (s_otv, s_otv.val)
                                if pend[0] is not None:
                                    pend[0]()
                                    pend[0] = None
                                LAST_PF = s_pF.val
                            with contextlib.ExitStack() as bsc:
                                bps = [bsc.enter_context(nc.psum_tensor(f"b{chunk}_{grp}_{i}", [64, 512], F32)) for i in range(2)]
                                for pi, p in enumerate(pairs):
                                    for u in range(2):
                                        op_ = opsum[2 * pi + u]
                                        bp = bps[u]
                                        w = [(s_pF, LAST_PF), (s_dn, s_dn.val), (s_o, s_o.val)]
                                        sc.add("scalar", lambda e, op_=op_: e.activation(
                                            out=dtmp[64:65, :], in_=op_[64:65, :], func=AF.Ln, bias=c_zero[64:65, :], scale=1.0),
                                            waits=w, incs=[(s_dn, 1)])
                                        sc.add("scalar", lambda e: e.activation(
                                            out=dtmp[64:65, :], in_=dtmp[64:65, :], func=AF.Exp, bias=c_zero[64:65, :], scale=c_m1[64:65, :]),
                                            waits=[(s_dn, s_dn.val)], incs=[(s_dn, 1)])
                                        DN = s_dn.val
                                        sc.add("tensor", lambda e, bp=bp: e.matmul(
                                            bp[:], onesb[64:65, :], dtmp[64:65, :],
                                            start=True, stop=True),
                                            waits=[(s_dn, DN)] + sc.guard(("bps", u)), incs=[(s_o, 1)])
                                        BC = s_o.val
                                        sc.add("vector", lambda e, op_=op_: e.tensor_copy(oscr[0:64, :], op_[0:64, :]),
                                               waits=[(s_pF, LAST_PF)] + sc.guard("oscr"), incs=[(s_on, 1)])
                                        EV = s_on.val
                                        col = 1024 * p + 512 * chunk
                                        if u == 0:
                                            sc.add("vector", lambda e, col=col, bp=bp: e.tensor_mul(
                                                onorm[0:64, col:col + 512], oscr[0:64, :], bp[:]),
                                                waits=[(s_on, EV), (s_o, BC)], incs=[(s_on, 1)])
                                            sc.set_guard("oscr", s_on)
                                            sc.set_guard(("bps", u), s_on)
                                        else:
                                            sc.add("vector", lambda e, bp=bp: e.tensor_mul(
                                                onS[0:64, :], oscr[0:64, :], bp[:]),
                                                waits=[(s_on, EV), (s_o, BC)] + sc.guard("onS"), incs=[(s_on, 1)])
                                            sc.set_guard("oscr", s_on)
                                            sc.set_guard(("bps", u), s_on)
                                            sc.add("gpsimd", lambda e, col=col: e.dma_start(
                                                out=onorm[64:128, col:col + 512], in_=onS[0:64, :]),
                                                waits=[(s_on, s_on.val)], incs=[(s_onD, 16)])
                                            sc.set_guard("onS", s_onD)
                                sc.set_guard("grp_done", s_on)
                emit_w2prep(1000)
                ONORM_DONE = s_on.val
                OND_DONE = s_onD.val
                emit_wstage(1000)
                W13_WAITS = [(s_cstS[0], s_cstS[0].val), (s_cstS[1], s_cstS[1].val)]

                ONORM_DBG_WAIT = [(s_dbg, s_dbg.val)] if "q" in debug else []
                if "onorm" in debug:
                    od = dbg_out("onorm_dbg", [128, 16 * 1024], BF16)
                    sc.add("sync", lambda e: e.dma_start(out=od[:], in_=onorm[:]),
                           waits=[(s_on, ONORM_DONE), (s_onD, OND_DONE)] + ONORM_DBG_WAIT, incs=[(s_dbg, 16)])
                    ONORM_DBG_WAIT = [(s_dbg, s_dbg.val)]

            W2_STAGED = s_w2s.val
            with contextlib.ExitStack() as bscope:
                wo_sb = sbR(bscope, "wo_sb", [128, 16 * 1024], BF16)
                rawst = sbR(bscope, "rawst", [128, 2 * 1024], F32)
                xbufsB = [sbR(bscope, f"xbB{i}", [128, D], F32) for i in range(2)]
                x2t = sbR(bscope, "x2t", [128, D], F32)
                xn2b = sbR(bscope, "xn2b", [128, D], BF16)
                xn2st = sbR(bscope, "xn2st", [128, 16 * 128], BF16)
                w2tr = sbR(bscope, "w2tr", [128, 4 * 2048], BF16)

                w2t_i = [0]

                def emit_w2t(n):
                    for _ in range(n):
                        j = w2t_i[0]
                        if j >= 64:
                            return
                        slot = j % 4
                        sw = s_w2tS[j % 4]
                        w = ([(s_w2s, W2_STAGED), (s_on, ONORM_DONE), (s_onD, OND_DONE)] + ONORM_DBG_WAIT
                             if j < 4 else sc.guard(("w2t", slot)))
                        sc.add("scalar", lambda e, j=j, slot=slot: e.dma_start(
                            out=w2tr[:, 2048 * slot:2048 * (slot + 1)],
                            in_=w2_bf[:, 128 * j:128 * (j + 1)], transpose=True),
                            waits=w, incs=[(sw, 16)])
                        sc.add("gpsimd", lambda e, j=j, slot=slot: e.dma_start(
                            out=w2T_bf[128 * j:128 * (j + 1), :],
                            in_=w2tr[:, 2048 * slot:2048 * (slot + 1)]),
                            waits=[(sw, sw.val)], incs=[(s_w2g[slot], 16)])
                        sc.set_guard(("w2t", slot), s_w2g[slot])
                        w2t_i[0] += 1

                bns_tile = {}
                with contextlib.ExitStack() as pscope:
                    opp = [pscope.enter_context(nc.psum_tensor(f"opp{i}", [128, 1024], F32)) for i in range(2)]
                    for wh in range(2):
                        for u in range(2):
                            for g in range(4):
                                sc.add("gpsimd", lambda e, wh=wh, u=u, g=g: e.dma_start(
                                    out=wo_sb[64 * u:64 * (u + 1), 4096 * g:4096 * (g + 1)].rearrange("p (i n) -> p i n", n=1024),
                                    in_=wo_d[256 * (2 * g + u):256 * (2 * g + u) + 256,
                                             1024 * wh:1024 * (wh + 1)].rearrange("(i x) n -> x i n", x=64)),
                                    waits=(sc.guard("wo") if (u == 0 and g == 0) else []) +
                                          ([(s_on, ONORM_DONE), (s_onD, OND_DONE)] if (wh == 0 and u == 0 and g == 0) else []),
                                    incs=[(s_wo, 16)])
                        WO_READY = s_wo.val
                        for tt in range(8):
                            op_ = opp[tt % 2]
                            for p in range(16):
                                for half in range(2):
                                    w = []
                                    if p == 0 and half == 0:
                                        w = [(s_wo, WO_READY), (s_on, ONORM_DONE), (s_onD, OND_DONE)] + sc.guard(("opp", tt % 2))
                                    sc.add("tensor", lambda e, p=p, tt=tt, half=half, op_=op_: e.matmul(
                                        op_[:, 512 * half:512 * (half + 1)],
                                        onorm[:, 1024 * p + 128 * tt:1024 * p + 128 * (tt + 1)],
                                        wo_sb[:, 1024 * p + 512 * half:1024 * p + 512 * (half + 1)],
                                        start=(p == 0), stop=(p == 15)),
                                        waits=w,
                                        incs=[(s_op, 1)] if (p == 15 and half == 1) else [])
                            for k in range(2):
                                sc.add("vector", lambda e, tt=tt, k=k, wh=wh, op_=op_: e.bn_stats(
                                    bnAll[:, 24 * tt + 12 * wh + 6 * k:24 * tt + 12 * wh + 6 * (k + 1)],
                                    op_[:, 512 * k:512 * (k + 1)]),
                                    waits=[(s_op, s_op.val)] if k == 0 else [],
                                    incs=[(s_bns, 1)] if k == 1 else [])
                            bns_tile[tt] = s_bns.val
                            sc.add("vector", lambda e, tt=tt, op_=op_: e.tensor_copy(
                                rawst[:, 1024 * (tt % 2):1024 * (tt % 2) + 1024], op_[:]),
                                waits=sc.guard(("rawst", tt % 2)) +
                                      ([(s_w2s, s_w2s.val)] if (wh == 0 and tt < 2) else []),
                                incs=[(s_b, 1)])
                            sc.set_guard(("opp", tt % 2), s_b)
                            srw = s_rw[tt % 2]
                            sc.add("sync", lambda e, tt=tt, wh=wh: e.dma_start(
                                out=raw_dram[128 * tt:128 * (tt + 1), 1024 * wh:1024 * (wh + 1)],
                                in_=rawst[:, 1024 * (tt % 2):1024 * (tt % 2) + 1024]),
                                waits=[(s_b, s_b.val)], incs=[(srw, 16)])
                            sc.set_guard(("rawst", tt % 2), srw)
                            emit_w2t(4)
                        sc.set_guard("wo", s_op)
                    emit_w2t(99)
                    W2T_WAITS = [(g, g.val) for g in s_w2g]
                    RAW_WAITS = [(s_rw[0], s_rw[0].val), (s_rw[1], s_rw[1].val)]

                with contextlib.ExitStack() as pscope:
                    trps = [pscope.enter_context(nc.psum_tensor(f"trc{i}", [128, 2048], BF16)) for i in range(2)]
                    for tt in range(8):
                        slot = tt % 2
                        xb = xbufsB[slot]
                        sxb = s_xs[slot]
                        sc.add("sync", lambda e, tt=tt, xb=xb: e.dma_start(
                            out=xb[:], in_=raw_dram[128 * tt:128 * (tt + 1), :]),
                            waits=RAW_WAITS + sc.guard(("xbB", slot)), incs=[(sxb, 16)])
                        sc.add("sync", lambda e, tt=tt: e.dma_start(
                            out=x2t[:], in_=x_own[128 * tt:128 * (tt + 1), :]),
                            waits=sc.guard("x2t") + sc.guard("x2tS") + (RAW_WAITS if tt == 0 else []),
                            incs=[(s_x2l, 16)])
                        XL = [(sxb, sxb.val), (s_x2l, s_x2l.val)]
                        sc.add("vector", lambda e, tt=tt: e.bn_aggr(bn2b[:], bnAll[:, 24 * tt:24 * (tt + 1)]),
                               waits=[(s_bns, bns_tile[tt])], incs=[(s_ln, 1)])
                        sc.add("scalar", lambda e: e.activation(out=lntb[:], in_=bn2b[:, 1:2], func=AF.Ln, bias=c_eps[:], scale=1.0),
                               waits=[(s_ln, s_ln.val)], incs=[(s_ln, 1)])
                        sc.add("scalar", lambda e: e.activation(out=rstdb[:], in_=lntb[:], func=AF.Exp, bias=c_zero[:], scale=c_mh[:]),
                               waits=[(s_ln, s_ln.val)], incs=[(s_ln, 1)])
                        sc.add("vector", lambda e, xb=xb: e.tensor_scalar(
                            xb[:], xb[:], bn2b[:, 0:1], rstdb[:], ALU.subtract, ALU.mult),
                            waits=[(s_ln, s_ln.val)] + XL, incs=[(s_ln, 1)])
                        sc.guards[("xbB", slot)] = (s_ln, s_ln.val + 1)
                        sc.add("vector", lambda e, xb=xb: e.tensor_add(x2t[:], x2t[:], xb[:]),
                               waits=[(s_ln, s_ln.val)], incs=[(s_ln, 1)])
                        X2R = s_ln.val
                        sc.add("sync", lambda e, tt=tt: e.dma_start(
                            out=x2_dram[128 * tt:128 * (tt + 1), :], in_=x2t[:]),
                            waits=[(s_ln, X2R)], incs=[(s_x2s, 16)])
                        sc.set_guard("x2tS", s_x2s)
                        for k in range(4):
                            sc.add("vector", lambda e, k=k: e.bn_stats(
                                bn6[:, 6 * k:6 * (k + 1)], x2t[:, 512 * k:512 * (k + 1)]),
                                waits=[(s_ln, X2R)] if k == 0 else [],
                                incs=[(s_bns, 1)] if k == 3 else [])
                        sc.add("vector", lambda e: e.bn_aggr(bn2[:], bn6[:]),
                               waits=[(s_bns, s_bns.val)], incs=[(s_ln, 1)])
                        sc.add("scalar", lambda e: e.activation(out=lnt[:], in_=bn2[:, 1:2], func=AF.Ln, bias=c_eps[:], scale=1.0),
                               waits=[(s_ln, s_ln.val)], incs=[(s_ln, 1)])
                        sc.add("scalar", lambda e: e.activation(out=rstd[:], in_=lnt[:], func=AF.Exp, bias=c_zero[:], scale=c_mh[:]),
                               waits=[(s_ln, s_ln.val)], incs=[(s_ln, 1)])
                        sc.add("vector", lambda e: e.tensor_scalar(
                            xn2b[:], x2t[:], bn2[:, 0:1], rstd[:], ALU.subtract, ALU.mult),
                            waits=[(s_ln, s_ln.val)], incs=[(s_ln, 1)])
                        sc.guards["x2t"] = (s_ln, s_ln.val)
                        VAP = s_ln.val
                        tp = trps[slot]
                        for k in range(16):
                            w = []
                            if k == 0:
                                w = [(s_ln, VAP)] + sc.guard(("trc", slot))
                                if tt == 0:
                                    w += [(s_op, s_op.val)]
                            sc.add("tensor", lambda e, k=k, tp=tp: e.transpose(
                                tp[:, 128 * k:128 * (k + 1)], xn2b[:, 128 * k:128 * (k + 1)], ident[:]),
                                waits=w, incs=[(s_tr, 1)] if k == 15 else [])
                        sc.add("vector", lambda e, tp=tp: e.tensor_copy(
                            xn2st[:].rearrange("p (k t) -> p k t", t=128), tp[:].rearrange("p (k t) -> p k t", t=128)),
                            waits=[(s_tr, s_tr.val)] + sc.guard("xn2st"), incs=[(s_b, 1)])
                        sc.set_guard(("trc", slot), s_b)
                        sc.add("sync", lambda e, tt=tt: e.dma_start(
                            out=xn2t_dram.rearrange("(k p) t -> p k t", p=128)[:, :, 128 * tt:128 * (tt + 1)],
                            in_=xn2st[:].rearrange("p (k t) -> p k t", t=128)),
                            waits=[(s_b, s_b.val)], incs=[(s_xnt, 16)])
                        sc.set_guard("xn2st", s_xnt)
                XNT_STORED = s_xnt.val
                X2S_STORED = s_x2s.val

                if "x2" in debug:
                    x2d = dbg_out("x2_dbg", [TOWN, D])
                    sc.add("sync", lambda e: e.dma_start(out=x2d[:], in_=x2_dram[:]),
                           waits=[(s_x2s, X2S_STORED), (s_dbg, s_dbg.val)], incs=[(s_dbg, 16)])

            qo_scope.close()

            with contextlib.ExitStack() as ffn:
                ffi = contextlib.ExitStack()
                w1t = sbR(ffi, "w1t", [128, 16 * 2048], BF16)
                w3t = sbR(ffi, "w3t", [128, 16 * 2048], BF16)
                xn2T = sbR(ffi, "xn2T", [128, 16 * 512], BF16)
                hid = sbR(ffi, "hid", [128, 16 * 512], BF16)
                sil = sbR(ffi, "sil", [128, 512], BF16)
                w2s = [sbR(ffi, f"w2s{i}", [128, 1024], BF16) for i in range(3)]
                fas = sbR(ffi, "fas", [128, 2 * 1024], F32)

                if True:
                    for th in range(2):
                        sc.add("sync", lambda e, th=th: e.dma_start(
                            out=xn2T[:].rearrange("p (k t) -> p k t", t=512),
                            in_=xn2t_dram.rearrange("(k p) t -> p k t", p=128)[:, :, 512 * th:512 * (th + 1)]),
                            waits=[(s_xnt, XNT_STORED), (s_x2s, X2S_STORED)] + sc.guard("xn2T"), incs=[(s_fx, 16)])
                        XN2_READY = s_fx.val
                        for hc in range(4):
                            for k in range(16):
                                w = []
                                if k == 0:
                                    w = list(W13_WAITS) + sc.guard("w1t")
                                    if th == 0 and hc == 0:
                                        w += [(s_xnt, XNT_STORED), (s_x2s, X2S_STORED)] + list(W2T_WAITS) + ONORM_DBG_WAIT
                                sc.add("scalar", lambda e, hc=hc, k=k: e.dma_start(
                                    out=w1t[:, 2048 * k:2048 * (k + 1)],
                                    in_=w1_bf[2048 * hc:2048 * (hc + 1), 128 * k:128 * (k + 1)], transpose=True),
                                    waits=w, incs=[(s_fw, 16)])
                                sc.add("scalar", lambda e, hc=hc, k=k: e.dma_start(
                                    out=w3t[:, 2048 * k:2048 * (k + 1)],
                                    in_=w3_bf[2048 * hc:2048 * (hc + 1), 128 * k:128 * (k + 1)], transpose=True),
                                    incs=[(s_fw, 16)])
                            FW_READY = s_fw.val
                            upsc = contextlib.ExitStack()
                            gps = [upsc.enter_context(nc.psum_tensor(f"gps{th}_{hc}_{i}", [128, 512], F32)) for i in range(2)]
                            for sub in range(16):
                                gp = gps[sub % 2]
                                for k in range(16):
                                    w = []
                                    if k == 0:
                                        w = [(s_fw, FW_READY), (s_fx, XN2_READY)] + sc.guard(("gps", sub % 2))
                                        if sub <= 1:
                                            w += sc.guard("gpsbank")
                                        if th == 0 and hc == 0 and sub == 0:
                                            w += [(s_b, s_b.val)]
                                    sc.add("tensor", lambda e, k=k, sub=sub, gp=gp: e.matmul(
                                        gp[:], w1t[:, 2048 * k + 128 * sub:2048 * k + 128 * (sub + 1)],
                                        xn2T[:, 512 * k:512 * (k + 1)],
                                        start=(k == 0), stop=(k == 15)),
                                        waits=w, incs=[(s_fp, 1)] if k == 15 else [])
                                sc.add("scalar", lambda e, gp=gp: e.activation(
                                    out=sil[:], in_=gp[:], func=AF.Silu),
                                    waits=[(s_fp, s_fp.val)] + sc.guard("sil"), incs=[(s_sil, 1)])
                                SIL = s_sil.val
                                for k in range(16):
                                    w = [(s_sil, SIL)] if k == 0 else []
                                    sc.add("tensor", lambda e, k=k, sub=sub, gp=gp: e.matmul(
                                        gp[:], w3t[:, 2048 * k + 128 * sub:2048 * k + 128 * (sub + 1)],
                                        xn2T[:, 512 * k:512 * (k + 1)],
                                        start=(k == 0), stop=(k == 15)),
                                        waits=w, incs=[(s_fp, 1)] if k == 15 else [])
                                sc.add("vector", lambda e, sub=sub, gp=gp: e.scalar_tensor_tensor(
                                    hid[:, 512 * sub:512 * (sub + 1)],
                                    gp[:], FWHT_SCALE, sil[:], ALU.mult, ALU.mult),
                                    waits=[(s_fp, s_fp.val)], incs=[(s_fh, 1)])
                                sc.set_guard("sil", s_fh)
                                sc.add("tensor", lambda e, sub=sub, gp=gp: e.matmul(
                                    gp[:], h128[:], hid[:, 512 * sub:512 * (sub + 1)],
                                    start=True, stop=True),
                                    waits=[(s_fh, s_fh.val)], incs=[(s_fp, 1)])
                                sc.add("vector", lambda e, sub=sub, gp=gp: e.tensor_copy(
                                    hid[:, 512 * sub:512 * (sub + 1)], gp[:]),
                                    waits=[(s_fp, s_fp.val)], incs=[(s_fh, 1)])
                                sc.set_guard(("gps", sub % 2), s_fh)
                            HID_READY = s_fh.val
                            upsc.close()
                            sc.set_guard("w1t", s_fh)
                            for dh in range(2):
                                with contextlib.ExitStack() as wsc:
                                    wps4 = [wsc.enter_context(nc.psum_tensor(f"wp{th}_{hc}_{dh}_{i}", [128, 1024], F32)) for i in range(4)]
                                    for sub in range(16):
                                        wsl = w2s[sub % 3]
                                        swr = s_w2r[sub % 3]
                                        w = (list(W2T_WAITS) if (th == 0 and hc == 0 and dh == 0 and sub == 0) else []) + sc.guard(("w2s", sub % 3))
                                        sc.add("sync", lambda e, hc=hc, dh=dh, sub=sub, wsl=wsl: e.dma_start(
                                            out=wsl[:],
                                            in_=w2T_bf[2048 * hc + 128 * sub:2048 * hc + 128 * (sub + 1), 1024 * dh:1024 * (dh + 1)]),
                                            waits=w, incs=[(swr, 16)])
                                        WND = swr.val
                                        for tt in range(4):
                                            wp = wps4[tt]
                                            for half in range(2):
                                                w = []
                                                if half == 0:
                                                    w = [(swr, WND)]
                                                    if sub == 0:
                                                        w += [(s_fh, HID_READY)] + sc.guard(("wps", tt))
                                                inc = []
                                                if sub == 15 and half == 1:
                                                    inc = [(s_fp, 1)]
                                                elif sub < 15 and tt == 3 and half == 1:
                                                    inc = [(s_wu, 1)]
                                                sc.add("tensor", lambda e, sub=sub, tt=tt, half=half, wp=wp, wsl=wsl: e.matmul(
                                                    wp[:, 512 * half:512 * (half + 1)],
                                                    hid[:, 512 * sub + 128 * tt:512 * sub + 128 * (tt + 1)],
                                                    wsl[:, 512 * half:512 * (half + 1)],
                                                    start=(sub == 0), stop=(sub == 15)),
                                                    waits=w, incs=inc)
                                        if sub == 15:
                                            sc.set_guard(("w2s", sub % 3), s_fp)
                                        else:
                                            sc.set_guard(("w2s", sub % 3), s_wu)
                                    for tt in range(4):
                                        gt = 4 * th + tt
                                        fslot = tt % 2
                                        wp = wps4[tt]
                                        sc.add("vector", lambda e, fslot=fslot, wp=wp: e.tensor_copy(
                                            fas[:, 1024 * fslot:1024 * (fslot + 1)], wp[:]),
                                            waits=[(s_fp, s_fp.val)] + sc.guard(("fas", fslot)), incs=[(s_fe, 1)])
                                        sc.set_guard(("wps", tt), s_fe)
                                        sfa = s_fab[fslot]
                                        sc.add("gpsimd", lambda e, gt=gt, dh=dh, fslot=fslot, hc=hc: e.dma_start(
                                            out=facc_dram[128 * gt:128 * (gt + 1), 1024 * dh:1024 * (dh + 1)],
                                            in_=fas[:, 1024 * fslot:1024 * (fslot + 1)],
                                            accum_op=(ALU.add if hc > 0 else ALU.bypass)),
                                            waits=[(s_fe, s_fe.val)], incs=[(sfa, 16)])
                                        sc.set_guard(("fas", fslot), sfa)
                            sc.set_guard("gpsbank", s_fe)
                        sc.set_guard("xn2T", s_fe)
                    FFN_WAITS = [(s_fab[0], s_fab[0].val), (s_fab[1], s_fab[1].val)]
                    ffi.close()
                    fst = sbR(ffn, "fst", [128, 2 * 2048], F32)
                    ffl = sbR(ffn, "ffl", [128, 2 * 2048], F32)

                    for gt in range(8):
                        sl = gt % 2
                        co = 2048 * sl
                        sfl = s_fstl if sl == 0 else s_fstl2
                        sff = s_ffl if sl == 0 else s_ffl2
                        sc.add("sync", lambda e, gt=gt, co=co: e.dma_start(
                            out=fst[:, co:co + 2048], in_=x2_dram[128 * gt:128 * (gt + 1), :]),
                            waits=([(s_x2s, X2S_STORED)] if gt == 0 else []) +
                                  (list(FFN_WAITS) if gt < 2 else []) + sc.guard(("fst", sl)),
                            incs=[(sfl, 16)])
                        sc.add("sync", lambda e, gt=gt, co=co: e.dma_start(
                            out=ffl[:, co:co + 2048], in_=facc_dram[128 * gt:128 * (gt + 1), :]),
                            waits=(list(FFN_WAITS) if gt < 2 else []) + sc.guard(("ffl", sl)),
                            incs=[(sff, 16)])
                        FSL = (sfl, sfl.val)
                        FFLV = (sff, sff.val)
                        for k in range(4):
                            sc.add("vector", lambda e, k=k, co=co: e.bn_stats(
                                bn6[:, 6 * k:6 * (k + 1)], ffl[:, co + 512 * k:co + 512 * (k + 1)]),
                                waits=([FFLV]) if k == 0 else [],
                                incs=[(s_bns, 1)] if k == 3 else [])
                        sc.add("vector", lambda e: e.bn_aggr(bn2[:], bn6[:]),
                               waits=[(s_bns, s_bns.val)], incs=[(s_ln, 1)])
                        sc.add("scalar", lambda e: e.activation(out=lnt[:], in_=bn2[:, 1:2], func=AF.Ln, bias=c_eps[:], scale=1.0),
                               waits=[(s_ln, s_ln.val)], incs=[(s_ln, 1)])
                        sc.add("scalar", lambda e: e.activation(out=rstd[:], in_=lnt[:], func=AF.Exp, bias=c_zero[:], scale=c_mh[:]),
                               waits=[(s_ln, s_ln.val)], incs=[(s_ln, 1)])
                        sc.add("vector", lambda e, co=co: e.tensor_scalar(
                            ffl[:, co:co + 2048], ffl[:, co:co + 2048], bn2[:, 0:1], rstd[:], ALU.subtract, ALU.mult),
                            waits=[(s_ln, s_ln.val)], incs=[(s_ln, 1)])
                        sc.guards[("ffl", sl)] = (s_ln, s_ln.val + 1)
                        sc.add("vector", lambda e, co=co: e.tensor_add(
                            fst[:, co:co + 2048], fst[:, co:co + 2048], ffl[:, co:co + 2048]),
                            waits=[(s_ln, s_ln.val), FSL], incs=[(s_ln, 1)])
                        syv = s_y if sl == 0 else s_y2
                        sc.add("sync", lambda e, gt=gt, co=co: e.dma_start(
                            out=y_own[128 * gt:128 * (gt + 1), :], in_=fst[:, co:co + 2048]),
                            waits=[(s_ln, s_ln.val)], incs=[(syv, 16)])
                        sc.set_guard(("fst", sl), syv)

                    sc.add("sync", None, waits=[(s_y, s_y.val), (s_y2, s_y2.val), (s_dbg, s_dbg.val), (s_otd, s_otd.val)])
                    sc.add("gpsimd", None, waits=[(s_y, s_y.val), (s_y2, s_y2.val)])
                    sc.add("tensor", None, waits=[(s_y, s_y.val), (s_y2, s_y2.val)])
                    sc.add("scalar", None, waits=[(s_y, s_y.val), (s_y2, s_y2.val)])
                    sc.add("vector", None, waits=[(s_y, s_y.val), (s_y2, s_y2.val)])

        sc.limit = TRUNC
        global LAST_SEQ
        LAST_SEQ = sc.seq
        with nc.Block() as block:
            block.sync(lambda e: sc.emit("sync", e))
            block.gpsimd(lambda e: sc.emit("gpsimd", e))
            block.tensor(lambda e: sc.emit("tensor", e))
            block.scalar(lambda e: sc.emit("scalar", e))
            block.vector(lambda e: sc.emit("vector", e))

    return nc, dbg


def kernel(**inputs):
    x = np.asarray(inputs["L0_attn_x"])
    wq = np.ascontiguousarray(np.asarray(inputs["L0_attn_Wq"], dtype=np.float32))
    wk = np.ascontiguousarray(np.asarray(inputs["L0_attn_Wk"], dtype=np.float32))
    wv = np.ascontiguousarray(np.asarray(inputs["L0_attn_Wv"], dtype=np.float32))
    wo = np.ascontiguousarray(np.asarray(inputs["L0_attn_Wo"], dtype=np.float32))
    w1 = np.ascontiguousarray(np.asarray(inputs["L0_ffn_W1"], dtype=np.float32))
    w2 = np.ascontiguousarray(np.asarray(inputs["L0_ffn_W2"], dtype=np.float32))
    w3 = np.ascontiguousarray(np.asarray(inputs["L0_ffn_W3"], dtype=np.float32))
    xf = np.ascontiguousarray(np.asarray(x, dtype=np.float32).reshape(S, D))

    nc, dbg = build_program(tuple(DEBUG))

    tril = np.triu(np.ones((128, 128), dtype=np.float32)).astype(BF)
    onesm = np.ones((128, 128), dtype=BF)
    zerom = np.zeros((128, 128), dtype=BF)

    in_maps = []
    for c in range(NCORES):
        idx = np.concatenate([np.arange(128 * g, 128 * (g + 1)) for g in own_tiles(c)])
        mk = np.empty((2, NB, 2, 128, 128), dtype=BF)
        for par in range(2):
            tau = c if par == 0 else 7 - c
            for j in range(NB):
                t = onesm if j < tau else (tril if j == tau else zerom)
                mk[par, j, 0] = t
                mk[par, j, 1] = t
        in_maps.append({
            "x_full": xf, "x_own": np.ascontiguousarray(xf[idx]),
            "Wq": wq, "Wk": wk, "Wv": wv, "Wo": wo,
            "W1": w1, "W2": w2, "W3": w3,
            "masks": np.ascontiguousarray(mk.reshape(2 * NB * 2 * 128, 128)),
        })

    res = run_bass_kernel_spmd(nc, in_maps, core_ids=list(range(NCORES)))
    out = np.empty((S, D), dtype=np.float32)
    for c in range(NCORES):
        idx = np.concatenate([np.arange(128 * g, 128 * (g + 1)) for g in own_tiles(c)])
        out[idx] = res.results[c]["y_own"]
    if DEBUG:
        kernel.last_results = res
    return out.reshape(1, S, D)
